# version 40
# baseline (speedup 1.0000x reference)
import math
from contextlib import ExitStack

import numpy as np
import ml_dtypes

import concourse.bass as bass
import concourse.mybir as mybir
from concourse.bass_utils import run_bass_kernel_spmd

F32 = mybir.dt.float32
BF16 = mybir.dt.bfloat16
AF = mybir.ActivationFunctionType
ALU = mybir.AluOpType
AX = mybir.AxisListType

ENGS = ("sync", "gpsimd", "tensor", "vector", "scalar")
ENG_SEM = {"tensor": "pe", "vector": "dve", "scalar": "act", "sync": "sp",
           "gpsimd": "gp"}


class Buf:

    def __init__(self, plan, name, inherit=(), reg=None):
        self.plan = plan
        self.name = name
        self.writer = None
        self.readers = []
        self.inherit = list(inherit)
        self.reg = reg

    def resolve_inherit(self):
        deps = []
        if self.inherit and self.reg is not None:
            for nm in self.inherit:
                for ib in self.reg.get(nm, ()):
                    if ib.writer is not None:
                        deps.append(ib.writer)
                    deps.extend(ib.readers)
        self.inherit = []
        return deps


class Plan:
    def __init__(self):
        self.ops = []
        self.final_waits = []
        self._last_dma = {}

    def buf(self, name, inherit=(), reg=None):
        return Buf(self, name, inherit, reg)

    def op(self, eng, build, reads=(), writes=(), dma=None, extra=()):
        deps = set(extra)
        why = {}
        for b in reads:
            if b.writer is not None:
                deps.add(b.writer)
                why[b.writer] = "r:" + b.name
        for b in writes:
            for r in b.readers:
                why[r] = "war:" + b.name
            deps.update(b.readers)
            if b.writer is not None:
                deps.add(b.writer)
                why[b.writer] = "waw:" + b.name
            for d in b.resolve_inherit():
                deps.add(d)
                why[d] = "inh:" + b.name
        oid = len(self.ops)
        if dma is not None:
            prev = self._last_dma.get(dma)
            if prev is not None:
                deps.add(prev)
            self._last_dma[dma] = oid
        deps.discard(oid)
        for d in deps:
            assert d < oid
        self.ops.append(dict(eng=eng, build=build, deps=deps, dma=dma,
                             why=why))
        for b in reads:
            b.readers.append(oid)
        for b in writes:
            b.writer = oid
            b.readers = []
        return oid

    def lower(self, nc, block, sems):
        ops = self.ops
        waits = []
        marked = set()
        for i, o in enumerate(ops):
            w = []
            for d in o["deps"]:
                dd = ops[d]
                if (dd["dma"] is None and dd["eng"] == o["eng"]
                        and o["eng"] == "tensor"):
                    continue
                w.append(d)
            waits.append(w)
            marked.update(w)
        for i, o in enumerate(ops):
            if o["dma"] is not None:
                marked.add(i)
        rank = {}
        cnt = {}
        semof = {}
        for i, o in enumerate(ops):
            s = o["dma"] if o["dma"] is not None else ENG_SEM[o["eng"]]
            semof[i] = s
            if i in marked:
                cnt[s] = cnt.get(s, 0) + 1
                rank[i] = cnt[s]
        self._waits = waits
        self._marked = marked
        self._rank = rank
        self._semof = semof
        self._cnt = cnt

        def section(eng, engobj):
            last = {}
            for i, o in enumerate(ops):
                if o["eng"] != eng:
                    continue
                tgt = {}
                for d in self._waits[i]:
                    s = self._semof[d]
                    v = self._rank[d] * (16 if ops[d]["dma"] is not None else 1)
                    tgt[s] = max(tgt.get(s, 0), v)
                for s in sorted(tgt):
                    v = tgt[s]
                    if last.get(s, 0) < v:
                        engobj.wait_ge(sems[s], v)
                        last[s] = v
                ins = o["build"](engobj)
                if i in self._marked:
                    s = self._semof[i]
                    ins.then_inc(sems[s], 16 if o["dma"] is not None else 1)
            if eng == "sync":
                for s in self.final_waits:
                    tot = self._cnt.get(s, 0) * 16
                    if tot > last.get(s, 0):
                        engobj.wait_ge(sems[s], tot)

        return section


class PoolAlloc:

    def __init__(self, base, size):
        self.free = [(base, size)]
        self.live = {}
        self.hist = []

    def _expire(self, phase):
        dead = [n for n, (o, s, d) in self.live.items() if d < phase]
        for n in dead:
            off, size, d = self.live.pop(n)
            self.hist.append((off, size, n))
            self.free.append((off, size))
        self.free.sort()
        out = []
        for off, size in self.free:
            if out and out[-1][0] + out[-1][1] == off:
                out[-1] = (out[-1][0], out[-1][1] + size)
            else:
                out.append((off, size))
        self.free = [tuple(x) for x in out]

    def alloc(self, name, size, birth, death, align=64, side="left"):
        self._expire(birth)
        size = (size + align - 1) // align * align
        blocks = list(enumerate(self.free))
        if side == "right":
            blocks = blocks[::-1]
        for idx, (off, fsz) in blocks:
            if side == "right":
                a_off = (off + fsz - size) // align * align
                if a_off < off:
                    continue
            else:
                a_off = (off + align - 1) // align * align
            if a_off + size <= off + fsz:
                pre = a_off - off
                post = off + fsz - (a_off + size)
                repl = []
                if pre:
                    repl.append((off, pre))
                if post:
                    repl.append((a_off + size, post))
                self.free[idx:idx + 1] = repl
                inherit = [nm for h_off, h_sz, nm in self.hist
                           if h_off < a_off + size and a_off < h_off + h_sz]
                self.live[name] = (a_off, size, death)
                return a_off, inherit
        raise MemoryError(f"pool alloc failed for {name} size {size}")


def build_nc(cfg):
    T = cfg["T"]; D = cfg["D"]; NH = cfg["NH"]; HF = cfg["HF"]
    DH = 64
    TC = T // 128
    DC = D // 128
    NP = NH // 2
    TQ2 = TC // 2
    HG = cfg.get("HG", 512)
    NG = HF // HG
    HGC = HG // 128
    HFC = HF // 128
    TN = (T + 511) // 512
    DN = (D + 511) // 512
    NW = min(512, T)
    DW = min(512, D)
    WAVE = 4
    QGC = 4 if TC % 4 == 0 else 2
    GWC = QGC * 128
    assert NH * DH == D and T % 256 == 0 and D % 128 == 0 and HF % HG == 0
    assert HG % 128 == 0

    nc = bass.Bass("TRN2", target_bir_lowering=False)

    dx = nc.dram_tensor("x", [T, D], F32, kind="ExternalInput")
    dwq = nc.dram_tensor("wq", [D, D], F32, kind="ExternalInput")
    dwk = nc.dram_tensor("wk", [D, D], F32, kind="ExternalInput")
    dwv = nc.dram_tensor("wv", [D, D], F32, kind="ExternalInput")
    dwo = nc.dram_tensor("wo", [D, D], F32, kind="ExternalInput")
    dw1 = nc.dram_tensor("w1", [HF, D], F32, kind="ExternalInput")
    dw2 = nc.dram_tensor("w2", [D, HF], F32, kind="ExternalInput")
    dbqT = nc.dram_tensor("bqT", [128, DC], F32, kind="ExternalInput")
    dbkT = nc.dram_tensor("bkT", [128, DC], F32, kind="ExternalInput")
    dbvT = nc.dram_tensor("bvT", [128, DC], F32, kind="ExternalInput")
    db1T = nc.dram_tensor("b1T", [128, HFC], F32, kind="ExternalInput")
    dbob = nc.dram_tensor("bob", [128, D], F32, kind="ExternalInput")
    db2b = nc.dram_tensor("b2b", [128, D], F32, kind="ExternalInput")
    dident = nc.dram_tensor("ident", [128, 128], BF16, kind="ExternalInput")
    dmaskt = nc.dram_tensor("maskt", [128, 128], BF16, kind="ExternalInput")
    dmaska = nc.dram_tensor("maska", [128, 128], F32, kind="ExternalInput")
    dones = nc.dram_tensor("onesall", [128, 128], BF16, kind="ExternalInput")
    dones2 = nc.dram_tensor("ones2", [128, 128], BF16, kind="ExternalInput")
    dy = nc.dram_tensor("y", [T, D], F32, kind="ExternalOutput")

    stack = ExitStack()
    plan = Plan()

    def sball(name, shape, dtype):
        return nc.alloc_sbuf_tensor(name, shape, dtype)

    ident = sball("identS", [128, 128], BF16)
    maskt = sball("masktS", [128, 128], BF16)
    maska = sball("maskaS", [128, 128], F32)
    onesall = sball("onesallS", [128, 128], BF16)
    ones2 = sball("ones2S", [128, 128], BF16)
    bqT = sball("bqTS", [128, DC], F32)
    bkT = sball("bkTS", [128, DC], F32)
    bvT = sball("bvTS", [128, DC], F32)
    bvTb = sball("bvTbS", [128, DC], BF16)
    b1T = sball("b1TS", [128, HFC], F32)
    bob = sball("bobS", [128, D], F32)
    b2b = sball("b2bS", [128, D], F32)
    bias_cols = sball("biascolS", [128, NH], F32)
    alpha2 = sball("alpha2S", [128, 2 * DC], F32)
    alpha1g = sball("alpha1gS", [128, 2 * HGC], F32)
    qkmax = sball("qkmaxS", [128, 2 * max(TN, 1)], F32)
    nsm = sball("nsmS", [128, 8], F32)
    negmb = sball("negmbS", [128, 1], BF16)
    invrow = sball("invrowS", [128, 512], F32)
    invrow2 = sball("invrow2S", [128, 512], F32)
    invrowb = sball("invrowbS", [128, 512], BF16)
    invrowb2 = sball("invrowb2S", [128, 512], BF16)
    valstmp = sball("valstmpS", [128, 512], BF16)
    valstmp2 = sball("valstmp2S", [128, 512], BF16)
    qacc = sball("qaccS", [128, 512], F32)
    bvwrow = sball("bvwrowS", [128, D], BF16)
    lnS = sball("lnS", [128, 16], F32)

    pool_bytes_name = "pool0"

    def _addr_of(name):
        for a in nc.m.functions[0].allocations:
            try:
                for m in a.memorylocations:
                    if m.name == name:
                        return m.addr
            except Exception:
                pass
        return None

    SZ = dict(
        wv=DC * D * 2, qp=2 * T * 2, kp=2 * T * 2,
        wqs=2 * DC * 128 * 2, wks=2 * DC * 128 * 2,
        x_f=DC * T * 2, V_ext=TC * NP * 130 * 2,
        vals=NP * T * 2, wo=DC * D * 2,
        pT=5 * 512 * 2, q2=2048 * 2, prescan=HF * 2,
        acc=TC * D * 4, x1_f=DC * T * 2, xchunk=2 * D * 4,
        xbf1=2 * D * 2, xbf2=3 * D * 2, sqscratch=D * 4,
        s1T=2 * DC * HG * 2, s2T=2 * HGC * D * 2, h=2 * HGC * T * 2,
        w1stag=HGC * D * 2, w2stag=DC * HG * 2, outstage=2 * D * 4,
    )
    LIFE = dict(
        wv=(0, 2), qp=(5, 5), kp=(5, 5), wqs=(5, 5), wks=(5, 5),
        x_f=(1, 5), V_ext=(2, 5),
        vals=(5, 6), wo=(5, 6), pT=(5, 5),
        q2=(5, 5), prescan=(5, 5), acc=(6, 8), x1_f=(6, 7),
        xchunk=(6, 6), xbf1=(1, 1), xbf2=(6, 6), sqscratch=(6, 8),
        s1T=(7, 7), s2T=(7, 7), h=(7, 7), w1stag=(7, 7), w2stag=(7, 7),
        outstage=(8, 8),
    )
    import os as _os2
    if _os2.environ.get("DEBUG_LIVES"):
        LIFE = {k: (v[0], 8) for k, v in LIFE.items()}
    SIDE = dict(pT="right", q2="right", prescan="right", wo="right",
                xchunk="right", xbf1="right", xbf2="right",
                sqscratch="right", w1stag="right", w2stag="right",
                outstage="right", qp="right", kp="right", wqs="right",
                wks="right", wv="right", vals="right")
    ALLOC_ORDER = ["wv", "xbf1", "x_f", "V_ext",
                   "vals", "wo", "pT", "q2", "prescan",
                   "qp", "kp", "wqs", "wks",
                   "acc", "x1_f",
                   "sqscratch", "xchunk", "xbf2", "s1T", "s2T", "h",
                   "w1stag", "w2stag", "outstage"]

    def _dry(sz):
        p = PoolAlloc(0, sz)
        try:
            for n in ALLOC_ORDER:
                p.alloc(n, SZ[n], LIFE[n][0], LIFE[n][1],
                        side=SIDE.get(n, "left"))
        except MemoryError:
            return False
        return True

    probe = nc.alloc_sbuf_tensor("poolprobe", [128, 64], mybir.dt.int8)
    probe_addr = _addr_of("poolprobe")
    limit = nc.sbuf_top - probe_addr - 64 - 4096
    POOL = None
    for sz in range(150 * 1024, limit + 1, 512):
        if _dry(sz):
            POOL = sz
            break
    assert POOL is not None, f"no pool size <= {limit} works"
    pool_h = nc.alloc_sbuf_tensor(pool_bytes_name, [128, POOL],
                                  mybir.dt.int8, align_bytes=64)
    pool_base = _addr_of(pool_bytes_name)
    assert pool_base is not None
    pa = PoolAlloc(pool_base, POOL)

    tensors = {}
    inherits = {}

    def palloc(name, shape, dtype, align=64):
        nbytes = shape[1] * (2 if dtype == BF16 else 4)
        assert nbytes == SZ[name], (name, nbytes, SZ[name])
        off, inh = pa.alloc(name, SZ[name], LIFE[name][0], LIFE[name][1],
                            align=align, side=SIDE.get(name, "left"))
        t = nc.alloc_sbuf_tensor_at(name, shape, dtype, offset=off)
        tensors[name] = t
        inherits[name] = inh
        return t

    bufreg = {}

    def pbuf(name):
        b = plan.buf(name, inherit=inherits.get(name, ()), reg=bufreg)
        bufreg.setdefault(name, []).append(b)
        return b

    wv_sb = palloc("wv", [128, DC * D], BF16)
    xbf1 = palloc("xbf1", [128, 2 * D], BF16)
    x_f = palloc("x_f", [128, DC * T], BF16)
    V_ext = palloc("V_ext", [128, TC * NP * 130], BF16)
    vals = palloc("vals", [128, NP * T], BF16)
    wo_sb = palloc("wo", [128, DC * D], BF16)
    pT = palloc("pT", [128, 5 * 512], BF16)
    q2 = palloc("q2", [128, 2048], BF16)
    prescan = palloc("prescan", [128, HF], BF16)
    qp = palloc("qp", [128, 2 * T], BF16)
    kp = palloc("kp", [128, 2 * T], BF16)
    wqs = palloc("wqs", [128, 2 * DC * 128], BF16)
    wks = palloc("wks", [128, 2 * DC * 128], BF16)
    acc = palloc("acc", [128, TC * D], F32)
    x1_f = palloc("x1_f", [128, DC * T], BF16)
    sqscratch = palloc("sqscratch", [128, D], F32)
    xchunk = palloc("xchunk", [128, 2 * D], F32)
    xbf2 = palloc("xbf2", [128, 3 * D], BF16)
    s1T = palloc("s1T", [128, 2 * DC * HG], BF16)
    s2T = palloc("s2T", [128, 2 * HGC * D], BF16)
    h_sb = palloc("h", [128, 2 * HGC * T], BF16)
    w1stag = palloc("w1stag", [128, HGC * D], BF16)
    w2stag = palloc("w2stag", [128, DC * HG], BF16)
    outstage = palloc("outstage", [128, 2 * D], F32)

    PB = [stack.enter_context(nc.psum_tensor(f"PB{i}", [128, 512], F32))
          for i in range(8)]

    sem_names = (list(ENG_SEM.values())
                 + ["d_init", "d_x0", "d_x1", "d_wv", "d_q0", "d_q1",
                    "d_k0", "d_k1",
                    "d_wo", "d_pw", "d_f0", "d_f1", "d_w1", "d_w2",
                    "d_o0", "d_o1"])
    sems = {n: stack.enter_context(nc.semaphore(n)) for n in sem_names}

    def pbuf2(name):
        bufs = [plan.buf(name + "_0", inherit=inherits.get(name, ()),
                         reg=bufreg),
                plan.buf(name + "_1", inherit=inherits.get(name, ()),
                         reg=bufreg)]
        bufreg.setdefault(name, []).extend(bufs)
        return bufs

    b_consts = plan.buf("consts")
    b_wv = pbuf("wv")
    b_xf = pbuf("x_f")
    b_xbf1 = pbuf2("xbf1")
    b_V = pbuf("V_ext")
    b_qp = pbuf2("qp"); b_kp = pbuf2("kp")
    b_wqs = pbuf2("wqs"); b_wks = pbuf2("wks")
    b_vals = pbuf("vals"); b_wo = pbuf("wo")
    b_pT = pbuf("pT"); b_q2 = pbuf("q2"); b_pre = pbuf("prescan")
    b_pTs = [plan.buf(f"pTs{c}", inherit=inherits.get("pT", ()),
                      reg=bufreg) for c in range(5)]
    bufreg.setdefault("pT", []).extend(b_pTs)
    b_x1f = pbuf("x1_f")
    b_acc = [plan.buf(f"acc{t}", inherit=inherits.get("acc", ()),
                      reg=bufreg) for t in range(TC)]
    bufreg.setdefault("acc", []).extend(b_acc)
    b_xc = pbuf2("xchunk")
    b_xbf2 = [plan.buf(f"xbf2_{i}", inherit=inherits.get("xbf2", ()),
                       reg=bufreg) for i in range(3)]
    bufreg.setdefault("xbf2", []).extend(b_xbf2)
    b_sq = pbuf("sqscratch")
    b_s1T = pbuf2("s1T")
    b_s2T = pbuf2("s2T")
    b_h = pbuf2("h")
    b_w1s = pbuf("w1stag"); b_w2s = pbuf("w2stag")
    b_out = pbuf2("outstage")
    b_PB = [plan.buf(f"PB{i}") for i in range(8)]
    b_biasc = plan.buf("bias_cols")
    b_a2 = plan.buf("alpha2")
    b_a1 = [plan.buf("alpha1g0"), plan.buf("alpha1g1")]
    b_qkmax = plan.buf("qkmax")
    b_nsm = plan.buf("nsm")
    b_negmb = plan.buf("negmb")
    b_invrow = [plan.buf("invrow"), plan.buf("invrow2")]
    b_invrowb = [plan.buf("invrowb"), plan.buf("invrowb2")]
    b_vtmp = [plan.buf("valstmp"), plan.buf("valstmp2")]
    b_qacc = plan.buf("qacc")
    b_bvw = plan.buf("bvwrow")
    b_bvTb = plan.buf("bvTb")
    b_ln2 = [plan.buf("lnS0"), plan.buf("lnS1")]
    b_bob = plan.buf("bob_buf")

    OP = plan.op

    for (dst, src) in [(ident, dident), (maskt, dmaskt), (maska, dmaska),
                       (onesall, dones),
                       (ones2, dones2), (bqT, dbqT), (bkT, dbkT),
                       (bvT, dbvT), (b1T, db1T), (b2b, db2b)]:
        OP("sync", (lambda d=dst, s=src: lambda e: e.dma_start(
            out=d[:, :], in_=s[:, :]))(), writes=[b_consts], dma="d_init")
    OP("sync", lambda e: e.dma_start(out=bob[:, :], in_=dbob[:, :]),
       writes=[b_bob], dma="d_init")

    def wrearr(d, cols):
        return d[:, :].rearrange("(c p) q -> p c q", p=128)

    def w3d(sb, cols=None):
        cols = D if cols is None else cols
        return sb[:, :].rearrange("p (c q) -> p c q", q=cols)

    for tc in range(TC):
        sl = tc % 2
        OP("gpsimd", (lambda tc=tc, sl=sl: lambda e: e.dma_start(
            out=xbf1[:, sl * D:(sl + 1) * D],
            in_=dx[tc * 128:(tc + 1) * 128, :]))(),
           writes=[b_xbf1[sl]], dma=f"d_x{sl}")
        bank = b_PB[tc % 2]
        pbt = PB[tc % 2]
        for c in range(DC):
            OP("tensor", (lambda tc=tc, sl=sl, c=c, pbt=pbt: lambda e:
                e.matmul(pbt[:, (c % 4) * 128:(c % 4) * 128 + 128],
                         xbf1[:, sl * D + c * 128: sl * D + c * 128 + 128],
                         ident[:, :], start=True, stop=True))(),
               reads=[b_xbf1[sl], b_consts], writes=[bank])
            if c % 4 == 3 or c == DC - 1:
                c0 = (c // 4) * 4
                n = c - c0 + 1
                OP("scalar", (lambda tc=tc, c0=c0, n=n, pbt=pbt: lambda e:
                    e.copy(x_f[:, :].rearrange("p (c t) -> p c t", t=T)
                           [:, c0:c0 + n, tc * 128:tc * 128 + 128],
                           pbt[:, :].rearrange("p (c t) -> p c t", t=128)
                           [:, 0:n, :]))(),
                   reads=[bank], writes=[b_xf])

    OP("gpsimd", lambda e: e.dma_start(out=w3d(wv_sb), in_=wrearr(dwv, D)),
       writes=[b_wv], dma="d_wv")
    OP("gpsimd", lambda e: e.dma_start(out=w3d(wo_sb), in_=wrearr(dwo, D)),
       writes=[b_wo], dma="d_wo")

    OP("vector", lambda e: e.memset(
        V_ext[:, :].rearrange("p (n s) -> p n s", s=65)[:, :, 64:65], 1.0),
       writes=[b_V])

    for tc in range(TC):
        par = tc % 2
        vb = [b_PB[2 * par], b_PB[2 * par + 1]]
        vp = [PB[2 * par], PB[2 * par + 1]]
        for c in range(DC):
            for j in range(NP):
                st = (c == 0 and j % 4 == 0)
                sp = (c == DC - 1 and (j % 4 == 3 or j == NP - 1))
                OP("tensor", (lambda tc=tc, c=c, j=j, vp=vp, st=st, sp=sp:
                    lambda e:
                    e.matmul(vp[j // 4][:, (j % 4) * 128:(j % 4) * 128 + 128],
                             x_f[:, c * T + tc * 128: c * T + tc * 128 + 128],
                             wv_sb[:, c * D + j * 128: c * D + j * 128
                                   + 128],
                             start=st, stop=sp, skip_group_check=True))(),
                   reads=[b_xf, b_wv],
                   writes=[vb[j // 4]])
        for j in range(NP):
            base = (tc * NP + j) * 130
            OP("vector", (lambda tc=tc, j=j, base=base, vp=vp: lambda e:
                e.tensor_copy(
                    V_ext[:, base:base + 130].rearrange(
                        "p (a b) -> p a b", b=65)[:, :, 0:64],
                    vp[j // 4][:, (j % 4) * 128:(j % 4) * 128 + 128]
                    .rearrange("p (a b) -> p a b", b=64)))(),
               reads=[vb[j // 4]], writes=[b_V])

    def prescan_dma(c):
        OP("gpsimd", (lambda c=c: lambda e: e.dma_start(
            out=prescan[:, :], in_=dw2[c * 128:(c + 1) * 128, :]))(),
           writes=[b_pre], dma="d_pw")

    def prescan_reduce(c, h):
        HH = HF // 2
        OP("vector", (lambda c=c, h=h, HH=HH: lambda e: e.tensor_reduce(
            alpha2[:, h * DC + c:h * DC + c + 1],
            prescan[:, h * HH:(h + 1) * HH], axis=AX.X, op=ALU.add,
            apply_absolute_value=True))(),
           reads=[b_pre], writes=[b_a2])


    QG = QGC
    GW = GWC
    WCH = max(1, 512 // GW)
    NU = TC // QG
    b_biasp = [plan.buf(f"biasc{j}") for j in range(NP)]

    def setup_items(j):
        sl = j % 2
        items = []
        for (srcb, src0, col0) in [(b_qp[sl], qp, 0), (b_kp[sl], kp, 1)]:
            def pre_sq(sl=sl, src0=src0, srcb=srcb):
                OP("vector", lambda e: e.tensor_mul(
                    q2[:, 0:T], src0[:, sl * T:(sl + 1) * T],
                    src0[:, sl * T:(sl + 1) * T]),
                   reads=[srcb], writes=[b_q2])
            items.append(("pre", pre_sq))
            for n in range(TN):
                w = min(512, T - n * 512)

                def mm_f(n=n, w=w):
                    def go(bi, slot):
                        OP("tensor", lambda e: e.matmul(
                            PB[bi][:, 0:w], ones2[:, :],
                            q2[:, n * 512:n * 512 + w],
                            start=True, stop=True),
                           reads=[b_q2, b_consts], writes=[b_PB[bi]])
                    return go

                def fin_f(n=n, w=w, col0=col0):
                    def go(bi, slot):
                        OP("vector", lambda e: e.reduce_max(
                            qkmax[:, col0 * TN + n:col0 * TN + n + 1],
                            PB[bi][:, 0:w], axis=AX.X),
                           reads=[b_PB[bi]], writes=[b_qkmax])
                    return go
                items.append(("task", mm_f(), fin_f()))

        def pre_chain_a(j=j):
            OP("vector", lambda e: e.reduce_max(
                nsm[:, 0:1], qkmax[:, 0:TN], axis=AX.X),
               reads=[b_qkmax], writes=[b_nsm])
            OP("vector", lambda e: e.reduce_max(
                nsm[:, 1:2], qkmax[:, TN:2 * TN], axis=AX.X),
               reads=[b_qkmax], writes=[b_nsm])
            OP("vector", lambda e: e.tensor_scalar(
                out=nsm[:, 2:3], in0=nsm[:, 0:1], scalar1=nsm[:, 1:2],
                scalar2=1e-30, op0=ALU.mult, op1=ALU.add),
               reads=[b_nsm], writes=[b_nsm])

        def pre_chain_b(j=j):
            OP("scalar", lambda e: e.activation(
                nsm[:, 3:4], nsm[:, 2:3], AF.Ln),
               reads=[b_nsm], writes=[b_nsm])
            OP("scalar", lambda e: e.activation(
                nsm[:, 4:5], nsm[:, 3:4], AF.Exp, scale=0.5),
               reads=[b_nsm], writes=[b_nsm])
            OP("vector", lambda e: e.tensor_scalar(
                out=negmb[:, 0:1], in0=nsm[:, 4:5], scalar1=-0.125,
                scalar2=None, op0=ALU.mult),
               reads=[b_nsm], writes=[b_negmb])
        items.append(("pre", pre_chain_a))
        items.append(("pre", pre_chain_b))

        def rep_mm(j=j):
            def go(bi, slot):
                def bb(e):
                    ins = None
                    for hp in range(2):
                        ins = e.matmul(
                            PB[bi][:, hp:hp + 1],
                            onesall[64 * hp:64 * hp + 1, 0:128],
                            negmb[64 * hp:64 * hp + 1, 0:1],
                            start=True, stop=True)
                    return ins
                OP("tensor", bb, reads=[b_negmb, b_consts],
                   writes=[b_PB[bi]])
            return go

        def rep_fin(j=j):
            def go(bi, slot):
                OP("vector", lambda e: e.tensor_copy(
                    bias_cols[:, 2 * j:2 * j + 2], PB[bi][:, 0:2]),
                   reads=[b_PB[bi]], writes=[b_biasp[j]])
            return go
        items.append(("task", rep_mm(), rep_fin()))
        return items

    def qk_items(j):
        sl = j % 2
        out = []
        for (wsl, bwsl, dst, bdst, bT) in [
                (wqs, b_wqs[sl], qp, b_qp[sl], bqT),
                (wks, b_wks[sl], kp, b_kp[sl], bkT)]:
            for n in range(TN):
                w = min(512, T - n * 512)

                halves = ([(0, DC, False)] if DC < 2 else
                          [(0, DC // 2, False), (DC // 2, DC, True)])
                for (c0, c1, fin_half) in halves:

                    def mk_mm(j=j, sl=sl, n=n, w=w, wsl=wsl, bwsl=bwsl,
                              c0=c0, c1=c1):
                        def go(bi, slot):
                            def bb(e):
                                ins = None
                                for c in range(c0, c1):
                                    ins = e.matmul(
                                        PB[bi][:, 0:w],
                                        wsl[:, sl * DC * 128 + c * 128:
                                            sl * DC * 128 + c * 128 + 128],
                                        x1f_src[:, c * T + n * 512:
                                                c * T + n * 512 + w],
                                        start=(c == c0),
                                        stop=(c == c1 - 1))
                                return ins
                            OP("tensor", bb, reads=[bwsl, b_xf],
                               writes=[b_PB[bi]])
                        return go

                    def mk_ev(j=j, sl=sl, n=n, w=w, dst=dst, bdst=bdst,
                              bT=bT, fin_half=fin_half, two=(DC >= 2)):
                        def go(bi, slot):
                            if two and not fin_half:
                                OP("vector", lambda e: e.tensor_copy(
                                    qacc[:, 0:w], PB[bi][:, 0:w]),
                                   reads=[b_PB[bi]], writes=[b_qacc])
                            elif two:
                                OP("vector", lambda e:
                                    e.scalar_tensor_tensor(
                                        out=dst[:, sl * T + n * 512:
                                                sl * T + n * 512 + w],
                                        in0=PB[bi][:, 0:w],
                                        scalar=bT[:, j:j + 1],
                                        in1=qacc[:, 0:w],
                                        op0=ALU.add, op1=ALU.add),
                                   reads=[b_PB[bi], b_consts, b_qacc],
                                   writes=[bdst])
                            else:
                                OP("vector", lambda e: e.tensor_scalar(
                                    out=dst[:, sl * T + n * 512:
                                            sl * T + n * 512 + w],
                                    in0=PB[bi][:, 0:w],
                                    scalar1=bT[:, j:j + 1],
                                    scalar2=None, op0=ALU.add),
                                   reads=[b_PB[bi], b_consts],
                                   writes=[bdst])
                        return go
                    out.append(("task", mk_mm(), mk_ev()))
        return out

    def wslice_load(j):
        sl = j % 2
        OP("gpsimd", (lambda j=j, sl=sl: lambda e: e.dma_start(
            out=wqs[:, sl * DC * 128:(sl + 1) * DC * 128].rearrange(
                "p (c q) -> p c q", q=128),
            in_=dwq[:, j * 128:(j + 1) * 128].rearrange(
                "(c p) q -> p c q", p=128)))(),
           writes=[b_wqs[sl]], dma=f"d_q{sl}")
        OP("gpsimd", (lambda j=j, sl=sl: lambda e: e.dma_start(
            out=wks[:, sl * DC * 128:(sl + 1) * DC * 128].rearrange(
                "p (c q) -> p c q", q=128),
            in_=dwk[:, j * 128:(j + 1) * 128].rearrange(
                "(c p) q -> p c q", p=128)))(),
           writes=[b_wks[sl]], dma=f"d_k{sl}")

    x1f_src = x_f
    wave_tasks = []
    pre_ops = {}

    def append_task(st_f, fin_f, epi_f, due_extra=1, lag=4):
        i = len(wave_tasks)
        bi = i % 4
        slot = i % 5
        if epi_f is not None and not isinstance(epi_f, tuple):
            epi_f = (epi_f,)
        wave_tasks.append((
            (lambda st_f=st_f, bi=bi, slot=slot: st_f(bi, slot)),
            (lambda fin_f=fin_f, bi=bi, slot=slot: fin_f(bi, slot)),
            epi_f if epi_f is None else
            tuple((lambda f=f, bi=bi, slot=slot: f(bi, slot))
                  for f in epi_f),
            due_extra, lag,
        ))
        return i

    pre_ops.setdefault(0, []).append(lambda: wslice_load(0))
    if NP > 1:
        pre_ops.setdefault(0, []).append(lambda: wslice_load(1))
    def append_dummy(n):
        for _ in range(n):
            append_task(lambda bi, slot: None, lambda bi, slot: None, None)

    from collections import deque

    def append_item(item):
        if item[0] == "pre":
            append_dummy(2)
            pre_ops.setdefault(len(wave_tasks), []).append(item[1])
        else:
            append_task(item[1], item[2], None, lag=2)

    for item in qk_items(0) + setup_items(0):
        append_item(item)
    append_dummy(4)

    for j in range(NP):
        inj = (deque(qk_items(j + 1) + setup_items(j + 1))
               if j + 1 < NP else deque())
        if j + 2 < NP:
            pre_ops.setdefault(len(wave_tasks) + 1, []).append(
                (lambda j=j: lambda: wslice_load(j + 2))())
        nw_j = 2 * sum((QG * (u + 1) + WCH - 1) // WCH for u in range(NU))
        lo_c, hi_c = j * DC // NP, (j + 1) * DC // NP
        nsub = 2 * (hi_c - lo_c) + 1
        for ci, c in enumerate(range(lo_c, hi_c)):
            off = max(3, (nw_j * (2 * ci + 1)) // (2 * nsub))
            pre_ops.setdefault(len(wave_tasks) + off, []).append(
                (lambda c=c: lambda: prescan_dma(c))())
            for h in range(2):
                offr = max(4, (nw_j * (2 * ci + 2 + h)) // (2 * nsub))
                pre_ops.setdefault(len(wave_tasks) + offr, []).append(
                    (lambda c=c, h=h: lambda: prescan_reduce(c, h))())
        import os as _os
        ntask = sum(1 for it in inj if it[0] == "task")
        stride = (10**9 if _os.environ.get('NOINJ') else
                  max(2, (2 * nw_j) // (3 * (ntask + 1)))) if inj else 0
        stride += stride % 2
        wcount = 0
        for u in range(NU):
            q0 = u * QG
            nch = q0 + QG
            nwav = (nch + WCH - 1) // WCH
            qcol = j * T + u * GW
            for w in range(nwav):
                l0 = w * WCH
                l1 = min(l0 + WCH, nch)
                qsl = j % 2

                def mk_st(l0=l0, l1=l1, j=j, qcol=qcol, q0=q0, qsl=qsl, u=u):
                    def go(bi0, bi1):
                        def bb(e):
                            ins = None
                            for l in range(l0, l1):
                                off = max(0, l - q0) * 128
                                for (hq, bb_) in ((0, bi0), (64, bi1)):
                                    ins = e.matmul(
                                        PB[bb_][:, (l - l0) * GW + off:
                                                (l - l0) * GW + GW],
                                        kp[hq:hq + 64,
                                           qsl * T + l * 128:
                                           qsl * T + l * 128 + 128],
                                        qp[hq:hq + 64,
                                           qsl * T + u * GW + off:
                                           qsl * T + u * GW + GW],
                                        start=True, stop=True)
                            return ins
                        OP("tensor", bb,
                           reads=[b_qp[qsl], b_kp[qsl]],
                           writes=[b_PB[bi0], b_PB[bi1]])
                    return go

                def mk_fin(l0=l0, l1=l1, j=j, q0=q0, nch=nch, qcol=qcol,
                           u=u, qsl=qsl, last=(w == nwav - 1)):
                    def go(hp, bi, slot):
                        st, stb = PB[bi], b_PB[bi]
                        pvx = 4 + hp + 2 * (u % 2)
                        pvp, pvb = PB[pvx], b_PB[pvx]
                        ptb = b_pTs[slot]
                        pt0 = slot * 512
                        regs = []
                        for l in range(l0, l1):
                            off = max(0, l - q0) * 128
                            if off == 0 and regs and regs[-1][1] == l:
                                regs[-1] = (regs[-1][0], l + 1, 0)
                            else:
                                regs.append((l, l + 1, off))
                        for (ra, rb, off) in regs:
                            OP("scalar", (lambda ra=ra, rb=rb, off=off,
                                          st=st, pt0=pt0, l0=l0, hp=hp:
                                lambda e: e.activation(
                                    pT[:, pt0 + (ra - l0) * GW + off:
                                       pt0 + (rb - l0) * GW],
                                    st[:, (ra - l0) * GW + off:
                                       (rb - l0) * GW],
                                    AF.Exp, scale=0.125,
                                    bias=bias_cols[:, 2 * j + hp:
                                                   2 * j + hp + 1]))(),
                               reads=[stb, b_biasp[j]], writes=[ptb])
                        for l in range(max(l0, q0), l1):
                            s = l - q0
                            OP("gpsimd", (lambda l=l, s=s, l0=l0, pt0=pt0:
                                lambda e: e.tensor_mul(
                                    pT[:, pt0 + (l - l0) * GW + s * 128:
                                       pt0 + (l - l0) * GW + s * 128 + 128],
                                    pT[:, pt0 + (l - l0) * GW + s * 128:
                                       pt0 + (l - l0) * GW + s * 128 + 128],
                                    maskt[:, :]))(),
                               reads=[b_consts], writes=[ptb])

                        def pv(e):
                            ins = None
                            for l in range(l0, l1):
                                off = max(0, l - q0) * 128
                                base = (l * NP + j) * 130 + hp * 65
                                ins = e.matmul(
                                    pvp[0:65, off:GW],
                                    V_ext[:, base:base + 65],
                                    pT[:, pt0 + (l - l0) * GW + off:
                                       pt0 + (l - l0) * GW + GW],
                                    start=(l == 0), stop=(l == nch - 1),
                                    skip_group_check=True)
                            return ins
                        OP("tensor", pv, reads=[b_V, ptb], writes=[pvb])

                    def epi_a(hp):
                        ir, irb, vt = (invrow, invrowb, valstmp) \
                            if hp == 0 else (invrow2, invrowb2, valstmp2)
                        pvx = 4 + hp + 2 * (u % 2)
                        pvp, pvb = PB[pvx], b_PB[pvx]
                        OP("vector", (lambda vt=vt, pvp=pvp: lambda e:
                            e.tensor_copy(
                                vt[0:64, 0:GW], pvp[0:64, 0:GW]))(),
                           reads=[pvb], writes=[b_vtmp[hp]])
                        OP("vector", (lambda ir=ir, pvp=pvp: lambda e:
                            e.reciprocal(
                                ir[64:65, 0:GW],
                                pvp[64:65, 0:GW]))(),
                           reads=[pvb], writes=[b_invrow[hp]])
                        OP("vector", (lambda ir=ir, irb=irb: lambda e:
                            e.tensor_copy(
                                irb[64:65, 0:GW], ir[64:65, 0:GW]))(),
                           reads=[b_invrow[hp]], writes=[b_invrowb[hp]])
                        OP("tensor", (lambda irb=irb, pvp=pvp: lambda e:
                            e.matmul(
                                pvp[0:64, 0:GW], onesall[64:65, 0:64],
                                irb[64:65, 0:GW], start=True, stop=True))(),
                           reads=[b_invrowb[hp], b_consts], writes=[pvb])

                    def epi_b(hp):
                        vt = valstmp if hp == 0 else valstmp2
                        pvx = 4 + hp + 2 * (u % 2)
                        pvp, pvb = PB[pvx], b_PB[pvx]
                        if hp == 0:
                            OP("vector", (lambda vt=vt, pvp=pvp: lambda e:
                                e.scalar_tensor_tensor(
                                    out=vals[0:64, qcol:qcol + GW],
                                    in0=vt[0:64, 0:GW], scalar=1.0,
                                    in1=pvp[0:64, 0:GW],
                                    op0=ALU.mult, op1=ALU.mult))(),
                               reads=[b_vtmp[hp], pvb], writes=[b_vals])
                        else:
                            OP("vector", (lambda vt=vt, pvp=pvp: lambda e:
                                e.scalar_tensor_tensor(
                                    out=vt[0:64, 0:GW],
                                    in0=vt[0:64, 0:GW], scalar=1.0,
                                    in1=pvp[0:64, 0:GW],
                                    op0=ALU.mult, op1=ALU.mult))(),
                               reads=[b_vtmp[hp], pvb],
                               writes=[b_vtmp[hp]])
                            OP("tensor", (lambda vt=vt, pvp=pvp: lambda e:
                                e.matmul(
                                    pvp[64:128, 0:GW], ident[0:64, 0:64],
                                    vt[0:64, 0:GW], start=True, stop=True,
                                    tile_position=(0, 64)))(),
                               reads=[b_vtmp[hp], b_consts], writes=[pvb])
                            OP("vector", (lambda pvp=pvp: lambda e:
                                e.tensor_copy(
                                    vals[64:128, qcol:qcol + GW],
                                    pvp[64:128, 0:GW]))(),
                               reads=[pvb], writes=[b_vals])
                    return go, ((epi_a, epi_b) if last else None)

                if u + 1 < NU:
                    nwav_next = (QG * (u + 2) + WCH - 1) // WCH
                else:
                    nwav_next = (QG + WCH - 1) // WCH
                due_extra = max(4, 2 * nwav_next - 2)
                fin_f, epi_f = mk_fin()
                i0 = len(wave_tasks)
                bi0, bi1 = i0 % 4, (i0 + 1) % 4
                st_pair = mk_st()
                append_task(lambda bi, slot: None,
                            (lambda fin_f=fin_f: lambda bi, slot:
                             fin_f(0, bi, slot))(),
                            None if epi_f is None else
                            tuple((lambda f=f: lambda bi, slot: f(0))()
                                  for f in epi_f), due_extra)
                append_task((lambda st_pair=st_pair, bi0=bi0, bi1=bi1:
                             lambda bi, slot: st_pair(bi0, bi1))(),
                            (lambda fin_f=fin_f: lambda bi, slot:
                             fin_f(1, bi, slot))(),
                            None if epi_f is None else
                            tuple((lambda f=f: lambda bi, slot: f(1))()
                                  for f in epi_f), due_extra)
                wcount += 2
                if inj and wcount % stride == 0:
                    append_item(inj.popleft())
                    while inj and inj[0][0] == "pre":
                        append_item(inj.popleft())
        if inj:
            while inj:
                append_item(inj.popleft())
            append_dummy(4)

    nwt = len(wave_tasks)
    epi_q = []

    def _fin(k, vi):
        wave_tasks[k][1]()
        epis = wave_tasks[k][2]
        if epis is not None:
            due = vi + wave_tasks[k][3]
            for si, f in enumerate(epis):
                epi_q.append((due - 2 * (len(epis) - 1 - si), f))

    def _flush(vi):
        i = 0
        while i < len(epi_q):
            if epi_q[i][0] <= vi:
                epi_q.pop(i)[1]()
            else:
                i += 1

    fin_due = {}
    for k in range(nwt):
        fin_due.setdefault(k + wave_tasks[k][4], []).append(k)
    last_due = max(fin_due) if fin_due else 0
    for i in range(max(nwt, last_due + 1)):
        for f in pre_ops.get(i, ()):
            f()
        for k in fin_due.get(i, ()):
            _fin(k, i)
        _flush(i)
        if i < nwt:
            wave_tasks[i][0]()
    for _, f in epi_q:
        f()

    OP("vector", lambda e: e.tensor_add(
        alpha2[:, 0:DC], alpha2[:, 0:DC], alpha2[:, DC:2 * DC]),
       reads=[b_a2], writes=[b_a2])
    OP("vector", lambda e: e.tensor_scalar(
        out=alpha2[:, 0:DC], in0=alpha2[:, 0:DC], scalar1=1.0 / HF,
        scalar2=None, op0=ALU.mult),
       reads=[b_a2], writes=[b_a2])

    OP("vector", lambda e: e.tensor_copy(bvTb[:, :], bvT[:, :]),
       reads=[b_consts], writes=[b_bvTb])
    for n in range(DN):
        w = min(512, D - n * 512)

        def mk_bvw(n=n, w=w):
            def go(e):
                ins = None
                for c in range(DC):
                    ins = e.matmul(
                        PB[6][0:1, 0:w], bvTb[:, c:c + 1],
                        wo_sb[:, c * D + n * 512: c * D + n * 512 + w],
                        start=(c == 0), stop=(c == DC - 1))
                return ins
            return go
        OP("tensor", mk_bvw(), reads=[b_bvTb, b_wo], writes=[b_PB[6]])
        OP("vector", (lambda n=n, w=w: lambda e: e.tensor_copy(
            bvwrow[0:1, n * 512:n * 512 + w], PB[6][0:1, 0:w]))(),
           reads=[b_PB[6]], writes=[b_bvw])
    for n in range(DN):
        w = min(512, D - n * 512)
        OP("tensor", (lambda n=n, w=w: lambda e: e.matmul(
            PB[6][:, 0:w], onesall[0:1, 0:128],
            bvwrow[0:1, n * 512:n * 512 + w], start=True, stop=True))(),
           reads=[b_bvw, b_consts], writes=[b_PB[6]])
        OP("vector", (lambda n=n, w=w: lambda e: e.tensor_add(
            bob[:, n * 512:n * 512 + w], bob[:, n * 512:n * 512 + w],
            PB[6][:, 0:w]))(),
           reads=[b_PB[6]], writes=[b_bob])


    def emit_lnstats(par, src, src_buf, mean_on_act=True):
        o = 8 * par
        bl = b_ln2[par]
        bs = b_sq
        if mean_on_act:
            OP("scalar", lambda e: e.activation(
                sqscratch[:, 0:D], src(), AF.Identity,
                accum_out=lnS[:, o:o + 1]),
               reads=[src_buf], writes=[bs, bl])
        else:
            OP("vector", lambda e: e.reduce_sum(
                lnS[:, o:o + 1], src(), axis=AX.X),
               reads=[src_buf], writes=[bl])
        OP("scalar", lambda e: e.activation(
            sqscratch[:, 0:D], src(), AF.Square,
            accum_out=lnS[:, o + 1:o + 2]),
           reads=[src_buf], writes=[bs, bl])
        OP("vector", lambda e: e.tensor_scalar(
            out=lnS[:, o + 2:o + 3], in0=lnS[:, o:o + 1], scalar1=1.0 / D,
            scalar2=None, op0=ALU.mult),
           reads=[bl], writes=[bl])
        OP("vector", lambda e: e.tensor_scalar(
            out=lnS[:, o + 3:o + 4], in0=lnS[:, o + 2:o + 3],
            scalar1=lnS[:, o + 2:o + 3], scalar2=None, op0=ALU.mult),
           reads=[bl], writes=[bl])
        OP("vector", lambda e: e.tensor_scalar(
            out=lnS[:, o + 4:o + 5], in0=lnS[:, o + 1:o + 2],
            scalar1=1.0 / D, scalar2=lnS[:, o + 3:o + 4], op0=ALU.mult,
            op1=ALU.subtract),
           reads=[bl], writes=[bl])
        OP("vector", lambda e: e.tensor_scalar(
            out=lnS[:, o + 5:o + 6], in0=lnS[:, o + 4:o + 5], scalar1=1e-5,
            scalar2=None, op0=ALU.add),
           reads=[bl], writes=[bl])
        OP("scalar", lambda e: e.activation(lnS[:, o + 6:o + 7],
                                            lnS[:, o + 5:o + 6], AF.Ln),
           reads=[bl], writes=[bl])
        OP("scalar", lambda e: e.activation(lnS[:, o + 7:o + 8],
                                            lnS[:, o + 6:o + 7], AF.Exp,
                                            scale=-0.5),
           reads=[bl], writes=[bl])
        return o, bl

    def emit_tr(tcq):
        s3 = tcq % 3
        for half in range(max(1, DC // 4)):
            bank = b_PB[4 + half % 2]
            pbt = PB[4 + half % 2]
            c0 = half * 4
            n = min(4, DC - c0)

            def mk_tr(s3=s3, c0=c0, n=n, pbt=pbt):
                def go(e):
                    ins = None
                    for k in range(n):
                        ins = e.matmul(
                            pbt[:, k * 128:k * 128 + 128],
                            xbf2[:, s3 * D + (c0 + k) * 128:
                                 s3 * D + (c0 + k) * 128 + 128],
                            ident[:, :], start=True, stop=True)
                    return ins
                return go
            OP("tensor", mk_tr(), reads=[b_xbf2[s3], b_consts],
               writes=[bank])
            OP("vector", (lambda tcq=tcq, c0=c0, n=n, pbt=pbt: lambda e:
                e.tensor_copy(x1_f[:, :].rearrange("p (c t) -> p c t", t=T)
                              [:, c0:c0 + n, tcq * 128:tcq * 128 + 128],
                              pbt[:, :].rearrange("p (c t) -> p c t", t=128)
                              [:, 0:n, :]))(),
               reads=[bank], writes=[b_x1f])

    for tcq in range(TC):
        sl = tcq % 2
        s3 = tcq % 3
        opair = (PB[2 * sl], PB[2 * sl + 1])
        obuf = (b_PB[2 * sl], b_PB[2 * sl + 1])
        OP("sync", (lambda tcq=tcq, sl=sl: lambda e: e.dma_start(
            out=xchunk[:, sl * D:(sl + 1) * D],
            in_=dx[tcq * 128:(tcq + 1) * 128, :]))(),
           writes=[b_xc[sl]], dma=f"d_f{sl}")
        OP("gpsimd", (lambda sl=sl: lambda e: e.tensor_add(
            xchunk[:, sl * D:(sl + 1) * D], xchunk[:, sl * D:(sl + 1) * D],
            bob[:, :]))(),
           reads=[b_xc[sl], b_bob], writes=[b_xc[sl]])

        def mk_o(tcq=tcq, opair=opair):
            def go(e):
                ins = None
                for c in range(DC):
                    for n in range(DN):
                        w = min(512, D - n * 512)
                        ins = e.matmul(
                            opair[n][:, 0:w],
                            vals[:, c * T + tcq * 128: c * T + tcq * 128
                                 + 128],
                            wo_sb[:, c * D + n * 512: c * D + n * 512
                                  + w],
                            start=(c == 0), stop=(c == DC - 1))
                return ins
            return go
        OP("tensor", mk_o(), reads=[b_vals, b_wo], writes=list(obuf[:DN]))
        if tcq >= 2:
            emit_tr(tcq - 2)
        for n in range(DN):
            w = min(512, D - n * 512)
            OP("vector", (lambda sl=sl, n=n, w=w, opair=opair: lambda e:
                e.tensor_add(
                    xchunk[:, sl * D + n * 512:sl * D + n * 512 + w],
                    xchunk[:, sl * D + n * 512:sl * D + n * 512 + w],
                    opair[n][:, 0:w]))(),
               reads=[b_xc[sl], obuf[n]], writes=[b_xc[sl]])
        o, bl = emit_lnstats(
            sl, (lambda sl=sl: lambda: xchunk[:, sl * D:(sl + 1) * D])(),
            b_xc[sl], mean_on_act=False)
        OP("vector", (lambda tcq=tcq, sl=sl, o=o: lambda e: e.tensor_scalar(
            out=acc[:, tcq * D:(tcq + 1) * D],
            in0=xchunk[:, sl * D:(sl + 1) * D], scalar1=lnS[:, o + 2:o + 3],
            scalar2=lnS[:, o + 7:o + 8], op0=ALU.subtract, op1=ALU.mult))(),
           reads=[b_xc[sl], bl], writes=[b_acc[tcq]])
        OP("vector", (lambda o=o: lambda e: e.tensor_scalar(
            out=lnS[:, o + 2:o + 3], in0=lnS[:, o + 2:o + 3],
            scalar1=lnS[:, o + 7:o + 8], scalar2=-1.0, op0=ALU.mult,
            op1=ALU.mult))(),
           reads=[bl], writes=[bl])
        OP("scalar", (lambda tcq=tcq, sl=sl, s3=s3, o=o: lambda e:
            e.activation(
                xbf2[:, s3 * D:(s3 + 1) * D], xchunk[:, sl * D:(sl + 1) * D],
                AF.Identity, bias=lnS[:, o + 2:o + 3],
                scale=lnS[:, o + 7:o + 8]))(),
           reads=[b_xc[sl], bl], writes=[b_xbf2[s3]])
    emit_tr(TC - 2)
    emit_tr(TC - 1)
    for tcq in range(TC):
        OP("vector", (lambda tcq=tcq: lambda e: e.tensor_add(
            acc[:, tcq * D:(tcq + 1) * D], acc[:, tcq * D:(tcq + 1) * D],
            b2b[:, :]))(),
           reads=[b_consts], writes=[b_acc[tcq]])

    def ffn_a1(g):
        sl = g % 2
        ab = sl * HGC
        OP("gpsimd", (lambda g=g: lambda e: e.dma_start(
            out=w1stag[:, :].rearrange("p (c d) -> p c d", d=D),
            in_=dw1[g * HG:(g + 1) * HG, :].rearrange(
                "(c p) d -> p c d", p=128)))(),
           writes=[b_w1s], dma="d_w1")
        for m in range(HGC):
            OP("vector", (lambda m=m, ab=ab: lambda e: e.tensor_reduce(
                alpha1g[:, ab + m:ab + m + 1], w1stag[:, m * D:(m + 1) * D],
                axis=AX.X, op=ALU.add, apply_absolute_value=True))(),
               reads=[b_w1s], writes=[b_a1[sl]])
        OP("vector", (lambda ab=ab: lambda e: e.tensor_scalar(
            out=alpha1g[:, ab:ab + HGC], in0=alpha1g[:, ab:ab + HGC],
            scalar1=1.0 / D, scalar2=None, op0=ALU.mult))(),
           reads=[b_a1[sl]], writes=[b_a1[sl]])
        OP("scalar", lambda e: e.activation(
            w1stag[:, :], w1stag[:, :], AF.Sign),
           reads=[b_w1s], writes=[b_w1s])
        OP("gpsimd", (lambda g=g: lambda e: e.dma_start(
            out=w2stag[:, :].rearrange("p (c h) -> p c h", h=HG),
            in_=dw2[:, :].rearrange("(c p) h -> p c h", p=128)
            [:, :, g * HG:(g + 1) * HG]))(),
           writes=[b_w2s], dma="d_w2")
        OP("scalar", lambda e: e.activation(
            w2stag[:, :], w2stag[:, :], AF.Sign),
           reads=[b_w2s], writes=[b_w2s])
        for c in range(DC):
            OP("vector", (lambda c=c: lambda e: e.tensor_scalar(
                out=w2stag[:, c * HG:(c + 1) * HG],
                in0=w2stag[:, c * HG:(c + 1) * HG],
                scalar1=alpha2[:, c:c + 1], scalar2=None, op0=ALU.mult))(),
               reads=[b_w2s, b_a2], writes=[b_w2s])

    def ffn_a2(g):
        sl = g % 2
        for c in range(DC):
            bank = b_PB[6 + c % 2]
            stp = PB[6 + c % 2]

            def mk_t1(c=c, stp=stp, sl=sl):
                def go(e):
                    ins = None
                    for m in range(HGC):
                        ins = e.matmul(
                            stp[:, m * 128:m * 128 + 128],
                            w1stag[:, m * D + c * 128: m * D + c * 128
                                   + 128],
                            ident[:, :], start=True, stop=True)
                    return ins
                return go
            OP("tensor", mk_t1(), reads=[b_w1s, b_consts], writes=[bank])
            OP("scalar", (lambda c=c, sl=sl, stp=stp: lambda e: e.copy(
                s1T[:, sl * DC * HG + c * HG: sl * DC * HG + c * HG
                    + HG],
                stp[:, 0:HG]))(),
               reads=[bank], writes=[b_s1T[sl]])
        for c in range(DC):
            bank = b_PB[6 + c % 2]
            stp = PB[6 + c % 2]

            def mk_t2(c=c, stp=stp):
                def go(e):
                    ins = None
                    for m in range(HGC):
                        ins = e.matmul(
                            stp[:, m * 128:m * 128 + 128],
                            w2stag[:, c * HG + m * 128: c * HG + m * 128
                                   + 128],
                            ident[:, :], start=True, stop=True)
                    return ins
                return go
            OP("tensor", mk_t2(), reads=[b_w2s, b_consts], writes=[bank])
            OP("scalar", (lambda c=c, sl=sl, stp=stp: lambda e: e.copy(
                s2T[:, :].rearrange("p (s m d) -> p s m d", s=2, m=HGC)
                [:, sl, :, c * 128:c * 128 + 128],
                stp[:, 0:HGC * 128].rearrange("p (m t) -> p m t", t=128)))(),
               reads=[bank], writes=[b_s2T[sl]])

    def ffn_mm1(g):
        sl = g % 2
        ab = sl * HGC
        for m in range(HGC):
            seta = (m % 2 == 0)
            banks = ([b_PB[0], b_PB[1], b_PB[2], b_PB[3]] if seta else
                     [b_PB[4], b_PB[5], b_PB[6], b_PB[7]])

            def psl(n, seta=seta):
                if seta:
                    return PB[n][:, 0:512]
                return PB[4 + n][:, 0:512]

            def mk_m1(m=m, sl=sl, seta=seta):
                def go(e):
                    ins = None
                    for c in range(DC):
                        for n in range(TN):
                            w = min(512, T - n * 512)
                            ins = e.matmul(
                                psl(n, seta)[:, 0:w],
                                s1T[:, sl * DC * HG + c * HG + m * 128:
                                    sl * DC * HG + c * HG + m * 128 + 128],
                                x1_f[:, c * T + n * 512: c * T + n * 512
                                     + w],
                                start=(c == 0), stop=(c == DC - 1))
                    return ins
                return go
            wb = sorted({id(banks[n]): banks[n] for n in range(TN)}.values(),
                        key=lambda b: b.name)
            OP("tensor", mk_m1(), reads=[b_s1T[sl], b_x1f], writes=wb)
            for n in range(TN):
                w = min(512, T - n * 512)
                OP("scalar", (lambda g=g, m=m, n=n, w=w, sl=sl, seta=seta,
                              ab=ab:
                    lambda e: e.activation(
                        h_sb[:, sl * HGC * T + m * T + n * 512:
                             sl * HGC * T + m * T + n * 512 + w],
                        psl(n, seta)[:, 0:w], AF.Gelu,
                        bias=b1T[:, g * HGC + m:g * HGC + m + 1],
                        scale=alpha1g[:, ab + m:ab + m + 1]))(),
                   reads=[banks[n], b_a1[sl], b_consts], writes=[b_h[sl]])

    def ffn_mm2(g, tail_cb=None):
        sl = g % 2
        for tc in range(TC):
            par = tc % 2
            obk = [b_PB[2 * par], b_PB[2 * par + 1]]
            opp = [PB[2 * par], PB[2 * par + 1]]

            def mk_m2(tc=tc, sl=sl, opp=opp):
                def go(e):
                    ins = None
                    for m in range(HGC):
                        for n in range(DN):
                            w = min(512, D - n * 512)
                            ins = e.matmul(
                                opp[n][:, 0:w],
                                h_sb[:, sl * HGC * T + m * T + tc * 128:
                                     sl * HGC * T + m * T + tc * 128 + 128],
                                s2T[:, sl * HGC * D + m * D + n * 512:
                                    sl * HGC * D + m * D + n * 512 + w],
                                start=(m == 0), stop=(m == HGC - 1))
                    return ins
                return go
            OP("tensor", mk_m2(), reads=[b_h[sl], b_s2T[sl]],
               writes=obk[:DN])
            for n in range(DN):
                w = min(512, D - n * 512)
                OP("vector", (lambda tc=tc, n=n, w=w, opp=opp: lambda e:
                    e.tensor_add(
                        acc[:, tc * D + n * 512: tc * D + n * 512 + w],
                        acc[:, tc * D + n * 512: tc * D + n * 512 + w],
                        opp[n][:, 0:w]))(),
                   reads=[obk[n]], writes=[b_acc[tc]])
            if tail_cb is not None:
                tail_cb(tc)

    def ln2_chunk(tc):
        sl = tc % 2
        o, bl = emit_lnstats(
            sl, (lambda tc=tc: lambda: acc[:, tc * D:(tc + 1) * D])(),
            b_acc[tc])
        OP("vector", (lambda tc=tc, sl=sl, o=o: lambda e: e.tensor_scalar(
            out=outstage[:, sl * D:(sl + 1) * D],
            in0=acc[:, tc * D:(tc + 1) * D], scalar1=lnS[:, o + 2:o + 3],
            scalar2=lnS[:, o + 7:o + 8], op0=ALU.subtract, op1=ALU.mult))(),
           reads=[b_acc[tc], bl], writes=[b_out[sl]])
        OP("sync", (lambda tc=tc, sl=sl: lambda e: e.dma_start(
            out=dy[tc * 128:(tc + 1) * 128, :],
            in_=outstage[:, sl * D:(sl + 1) * D]))(),
           reads=[b_out[sl]], dma=f"d_o{sl}")

    ffn_a1(0)
    ffn_a2(0)
    for g in range(NG):
        if g + 1 < NG:
            ffn_a1(g + 1)
        ffn_mm1(g)
        if g + 1 < NG:
            ffn_a2(g + 1)
        ffn_mm2(g, tail_cb=((lambda tc: ln2_chunk(tc - 2) if tc >= 2
                             else None) if g == NG - 1 else None))
    ln2_chunk(TC - 2)
    ln2_chunk(TC - 1)

    plan.final_waits = ["d_o0", "d_o1"]

    with nc.Block() as block:
        section = plan.lower(nc, block, sems)

        @block.sync
        def _(e):
            section("sync", e)

        @block.gpsimd
        def _(e):
            section("gpsimd", e)

        @block.tensor
        def _(e):
            section("tensor", e)

        @block.vector
        def _(e):
            section("vector", e)

        @block.scalar
        def _(e):
            section("scalar", e)

    stack.close()
    return nc


def host_consts(cfg):
    D = cfg["D"]; HF = cfg["HF"]; DC = D // 128; HFC = HF // 128
    bf = ml_dtypes.bfloat16
    ident = np.eye(128, dtype=bf)
    maskt = np.triu(np.ones((128, 128), dtype=np.float32)).astype(bf)
    maska = np.where(np.triu(np.ones((128, 128), dtype=np.float32)) > 0,
                     0.0, -1e9).astype(np.float32)
    onesall = np.ones((128, 128), dtype=bf)
    ones2 = np.zeros((128, 128), dtype=bf)
    ones2[0:64, 0] = 1
    ones2[64:128, 64] = 1
    return dict(ident=ident, maskt=maskt, maska=maska,
                onesall=onesall, ones2=ones2)


def make_in_maps(cfg, x, wq, bq, wk, bk, wv, bv, wo, bo, w1, b1, w2, b2):
    D = cfg["D"]; HF = cfg["HF"]; DC = D // 128; HFC = HF // 128
    n_cores = x.shape[0]
    f = np.ascontiguousarray
    cst = host_consts(cfg)
    shared = dict(
        wq=f(wq, dtype=np.float32), wk=f(wk, dtype=np.float32),
        wv=f(wv, dtype=np.float32), wo=f(wo, dtype=np.float32),
        w1=f(w1, dtype=np.float32), w2=f(w2, dtype=np.float32),
        bqT=f(np.asarray(bq, np.float32).reshape(DC, 128).T),
        bkT=f(np.asarray(bk, np.float32).reshape(DC, 128).T),
        bvT=f(np.asarray(bv, np.float32).reshape(DC, 128).T),
        b1T=f(np.asarray(b1, np.float32).reshape(HFC, 128).T),
        bob=f(np.broadcast_to(np.asarray(bo, np.float32), (128, D))),
        b2b=f(np.broadcast_to(np.asarray(b2, np.float32), (128, D))),
        **cst,
    )
    return [dict(x=f(x[i], dtype=np.float32), **shared)
            for i in range(n_cores)]


def kernel(L0_attn_x, L0_attn_W_q, L0_attn_b_q, L0_attn_W_k, L0_attn_b_k,
           L0_attn_W_v, L0_attn_b_v, L0_attn_W_o, L0_attn_b_o,
           L0_ffn_w1, L0_ffn_b1, L0_ffn_w2, L0_ffn_b2):
    cfg = dict(T=2048, D=1024, NH=16, HF=4096, HG=512)
    x = np.asarray(L0_attn_x, dtype=np.float32)
    n_cores = x.shape[0]
    nc = build_nc(cfg)
    in_maps = make_in_maps(cfg, x, L0_attn_W_q, L0_attn_b_q, L0_attn_W_k,
                           L0_attn_b_k, L0_attn_W_v, L0_attn_b_v,
                           L0_attn_W_o, L0_attn_b_o, L0_ffn_w1, L0_ffn_b1,
                           L0_ffn_w2, L0_ffn_b2)
    res = run_bass_kernel_spmd(nc, in_maps, core_ids=list(range(n_cores)))
    return np.stack([np.asarray(r["y"], dtype=np.float32)
                     for r in res.results])


# version 41
# speedup vs baseline: 1.0777x; 1.0777x over previous
import math
from contextlib import ExitStack

import numpy as np
import ml_dtypes

import concourse.bass as bass
import concourse.mybir as mybir
from concourse.bass_utils import run_bass_kernel_spmd

F32 = mybir.dt.float32
BF16 = mybir.dt.bfloat16
AF = mybir.ActivationFunctionType
ALU = mybir.AluOpType
AX = mybir.AxisListType

ENGS = ("sync", "gpsimd", "tensor", "vector", "scalar")
ENG_SEM = {"tensor": "pe", "vector": "dve", "scalar": "act", "sync": "sp",
           "gpsimd": "gp"}


class Buf:

    def __init__(self, plan, name, inherit=(), reg=None):
        self.plan = plan
        self.name = name
        self.writer = None
        self.readers = []
        self.inherit = list(inherit)
        self.reg = reg

    def resolve_inherit(self):
        deps = []
        if self.inherit and self.reg is not None:
            for nm in self.inherit:
                for ib in self.reg.get(nm, ()):
                    if ib.writer is not None:
                        deps.append(ib.writer)
                    deps.extend(ib.readers)
        self.inherit = []
        return deps


class Plan:
    def __init__(self):
        self.ops = []
        self.final_waits = []
        self._last_dma = {}

    def buf(self, name, inherit=(), reg=None):
        return Buf(self, name, inherit, reg)

    def op(self, eng, build, reads=(), writes=(), dma=None, extra=()):
        deps = set(extra)
        why = {}
        for b in reads:
            if b.writer is not None:
                deps.add(b.writer)
                why[b.writer] = "r:" + b.name
        for b in writes:
            for r in b.readers:
                why[r] = "war:" + b.name
            deps.update(b.readers)
            if b.writer is not None:
                deps.add(b.writer)
                why[b.writer] = "waw:" + b.name
            for d in b.resolve_inherit():
                deps.add(d)
                why[d] = "inh:" + b.name
        oid = len(self.ops)
        if dma is not None:
            prev = self._last_dma.get(dma)
            if prev is not None:
                deps.add(prev)
            self._last_dma[dma] = oid
        deps.discard(oid)
        for d in deps:
            assert d < oid
        self.ops.append(dict(eng=eng, build=build, deps=deps, dma=dma,
                             why=why))
        for b in reads:
            b.readers.append(oid)
        for b in writes:
            b.writer = oid
            b.readers = []
        return oid

    def lower(self, nc, block, sems):
        ops = self.ops
        waits = []
        marked = set()
        for i, o in enumerate(ops):
            w = []
            for d in o["deps"]:
                dd = ops[d]
                if (dd["dma"] is None and dd["eng"] == o["eng"]
                        and o["eng"] == "tensor"):
                    continue
                w.append(d)
            waits.append(w)
            marked.update(w)
        for i, o in enumerate(ops):
            if o["dma"] is not None:
                marked.add(i)
        rank = {}
        cnt = {}
        semof = {}
        for i, o in enumerate(ops):
            s = o["dma"] if o["dma"] is not None else ENG_SEM[o["eng"]]
            semof[i] = s
            if i in marked:
                cnt[s] = cnt.get(s, 0) + 1
                rank[i] = cnt[s]
        self._waits = waits
        self._marked = marked
        self._rank = rank
        self._semof = semof
        self._cnt = cnt

        def section(eng, engobj):
            last = {}
            for i, o in enumerate(ops):
                if o["eng"] != eng:
                    continue
                tgt = {}
                for d in self._waits[i]:
                    s = self._semof[d]
                    v = self._rank[d] * (16 if ops[d]["dma"] is not None else 1)
                    tgt[s] = max(tgt.get(s, 0), v)
                for s in sorted(tgt):
                    v = tgt[s]
                    if last.get(s, 0) < v:
                        engobj.wait_ge(sems[s], v)
                        last[s] = v
                ins = o["build"](engobj)
                if i in self._marked:
                    s = self._semof[i]
                    ins.then_inc(sems[s], 16 if o["dma"] is not None else 1)
            if eng == "sync":
                for s in self.final_waits:
                    tot = self._cnt.get(s, 0) * 16
                    if tot > last.get(s, 0):
                        engobj.wait_ge(sems[s], tot)

        return section


class PoolAlloc:

    def __init__(self, base, size):
        self.free = [(base, size)]
        self.live = {}
        self.hist = []

    def _expire(self, phase):
        dead = [n for n, (o, s, d) in self.live.items() if d < phase]
        for n in dead:
            off, size, d = self.live.pop(n)
            self.hist.append((off, size, n))
            self.free.append((off, size))
        self.free.sort()
        out = []
        for off, size in self.free:
            if out and out[-1][0] + out[-1][1] == off:
                out[-1] = (out[-1][0], out[-1][1] + size)
            else:
                out.append((off, size))
        self.free = [tuple(x) for x in out]

    def alloc(self, name, size, birth, death, align=64, side="left"):
        self._expire(birth)
        size = (size + align - 1) // align * align
        blocks = list(enumerate(self.free))
        if side == "right":
            blocks = blocks[::-1]
        for idx, (off, fsz) in blocks:
            if side == "right":
                a_off = (off + fsz - size) // align * align
                if a_off < off:
                    continue
            else:
                a_off = (off + align - 1) // align * align
            if a_off + size <= off + fsz:
                pre = a_off - off
                post = off + fsz - (a_off + size)
                repl = []
                if pre:
                    repl.append((off, pre))
                if post:
                    repl.append((a_off + size, post))
                self.free[idx:idx + 1] = repl
                inherit = [nm for h_off, h_sz, nm in self.hist
                           if h_off < a_off + size and a_off < h_off + h_sz]
                self.live[name] = (a_off, size, death)
                return a_off, inherit
        raise MemoryError(f"pool alloc failed for {name} size {size}")


def build_nc(cfg):
    T = cfg["T"]; D = cfg["D"]; NH = cfg["NH"]; HF = cfg["HF"]
    DH = 64
    TC = T // 128
    DC = D // 128
    NP = NH // 2
    TQ2 = TC // 2
    HG = cfg.get("HG", 512)
    NG = HF // HG
    HGC = HG // 128
    HFC = HF // 128
    TN = (T + 511) // 512
    DN = (D + 511) // 512
    NW = min(512, T)
    DW = min(512, D)
    WAVE = 4
    QGC = 4 if TC % 4 == 0 else 2
    GWC = QGC * 128
    assert NH * DH == D and T % 256 == 0 and D % 128 == 0 and HF % HG == 0
    assert HG % 128 == 0

    nc = bass.Bass("TRN2", target_bir_lowering=False)

    dx = nc.dram_tensor("x", [T, D], F32, kind="ExternalInput")
    dwq = nc.dram_tensor("wq", [D, D], F32, kind="ExternalInput")
    dwk = nc.dram_tensor("wk", [D, D], F32, kind="ExternalInput")
    dwv = nc.dram_tensor("wv", [D, D], F32, kind="ExternalInput")
    dwo = nc.dram_tensor("wo", [D, D], F32, kind="ExternalInput")
    dw1 = nc.dram_tensor("w1", [HF, D], F32, kind="ExternalInput")
    dw2 = nc.dram_tensor("w2", [D, HF], F32, kind="ExternalInput")
    dbqT = nc.dram_tensor("bqT", [128, DC], F32, kind="ExternalInput")
    dbkT = nc.dram_tensor("bkT", [128, DC], F32, kind="ExternalInput")
    dbvT = nc.dram_tensor("bvT", [128, DC], F32, kind="ExternalInput")
    db1T = nc.dram_tensor("b1T", [128, HFC], F32, kind="ExternalInput")
    dbob = nc.dram_tensor("bob", [128, D], F32, kind="ExternalInput")
    db2b = nc.dram_tensor("b2b", [128, D], F32, kind="ExternalInput")
    dident = nc.dram_tensor("ident", [128, 128], BF16, kind="ExternalInput")
    dmaskt = nc.dram_tensor("maskt", [128, 128], BF16, kind="ExternalInput")
    dmaska = nc.dram_tensor("maska", [128, 128], F32, kind="ExternalInput")
    dones = nc.dram_tensor("onesall", [128, 128], BF16, kind="ExternalInput")
    dones2 = nc.dram_tensor("ones2", [128, 128], BF16, kind="ExternalInput")
    dy = nc.dram_tensor("y", [T, D], F32, kind="ExternalOutput")

    stack = ExitStack()
    plan = Plan()

    def sball(name, shape, dtype):
        return nc.alloc_sbuf_tensor(name, shape, dtype)

    ident = sball("identS", [128, 128], BF16)
    maskt = sball("masktS", [128, 128], BF16)
    maska = sball("maskaS", [128, 128], F32)
    onesall = sball("onesallS", [128, 128], BF16)
    ones2 = sball("ones2S", [128, 128], BF16)
    bqT = sball("bqTS", [128, DC], F32)
    bkT = sball("bkTS", [128, DC], F32)
    bvT = sball("bvTS", [128, DC], F32)
    bvTb = sball("bvTbS", [128, DC], BF16)
    b1T = sball("b1TS", [128, HFC], F32)
    bob = sball("bobS", [128, D], F32)
    b2b = sball("b2bS", [128, D], F32)
    bias_cols = sball("biascolS", [128, NH], F32)
    alpha2 = sball("alpha2S", [128, 2 * DC], F32)
    alpha1g = sball("alpha1gS", [128, 2 * HGC], F32)
    qkmax = sball("qkmaxS", [128, 2 * max(TN, 1)], F32)
    nsm = sball("nsmS", [128, 8], F32)
    negmb = sball("negmbS", [128, 1], BF16)
    invrow = sball("invrowS", [128, 512], F32)
    invrow2 = sball("invrow2S", [128, 512], F32)
    invrowb = sball("invrowbS", [128, 512], BF16)
    invrowb2 = sball("invrowb2S", [128, 512], BF16)
    valstmp = sball("valstmpS", [128, 512], BF16)
    valstmp2 = sball("valstmp2S", [128, 512], BF16)
    qacc = sball("qaccS", [128, 512], F32)
    bvwrow = sball("bvwrowS", [128, D], BF16)
    lnS = sball("lnS", [128, 16], F32)

    pool_bytes_name = "pool0"

    def _addr_of(name):
        for a in nc.m.functions[0].allocations:
            try:
                for m in a.memorylocations:
                    if m.name == name:
                        return m.addr
            except Exception:
                pass
        return None

    SZ = dict(
        wv=DC * D * 2, qp=2 * T * 2, kp=2 * T * 2,
        wqs=2 * DC * 128 * 2, wks=2 * DC * 128 * 2,
        x_f=DC * T * 2, V_ext=TC * NP * 130 * 2,
        vals=NP * T * 2, wo=DC * D * 2,
        pT=5 * 512 * 2, q2=2048 * 2, prescan=HF * 2,
        acc=TC * D * 4, x1_f=DC * T * 2, xchunk=2 * D * 4,
        xbf1=2 * D * 2, xbf2=3 * D * 2, sqscratch=D * 4,
        s1T=2 * DC * HG * 2, s2T=2 * HGC * D * 2, h=2 * HGC * T * 2,
        w1stag=HGC * D * 2, w2stag=DC * HG * 2, outstage=2 * D * 4,
    )
    LIFE = dict(
        wv=(0, 2), qp=(5, 5), kp=(5, 5), wqs=(5, 5), wks=(5, 5),
        x_f=(1, 5), V_ext=(2, 5),
        vals=(5, 6), wo=(5, 6), pT=(5, 5),
        q2=(5, 5), prescan=(5, 5), acc=(6, 8), x1_f=(6, 7),
        xchunk=(6, 6), xbf1=(1, 1), xbf2=(6, 6), sqscratch=(6, 8),
        s1T=(7, 7), s2T=(7, 7), h=(7, 7), w1stag=(7, 7), w2stag=(7, 7),
        outstage=(8, 8),
    )
    import os as _os2
    if _os2.environ.get("DEBUG_LIVES"):
        LIFE = {k: (v[0], 8) for k, v in LIFE.items()}
    SIDE = dict(pT="right", q2="right", prescan="right", wo="right",
                xchunk="right", xbf1="right", xbf2="right",
                sqscratch="right", w1stag="right", w2stag="right",
                outstage="right", qp="right", kp="right", wqs="right",
                wks="right", wv="right", vals="right")
    ALLOC_ORDER = ["wv", "xbf1", "x_f", "V_ext",
                   "vals", "wo", "pT", "q2", "prescan",
                   "qp", "kp", "wqs", "wks",
                   "acc", "x1_f",
                   "sqscratch", "xchunk", "xbf2", "s1T", "s2T", "h",
                   "w1stag", "w2stag", "outstage"]

    def _dry(sz):
        p = PoolAlloc(0, sz)
        try:
            for n in ALLOC_ORDER:
                p.alloc(n, SZ[n], LIFE[n][0], LIFE[n][1],
                        side=SIDE.get(n, "left"))
        except MemoryError:
            return False
        return True

    probe = nc.alloc_sbuf_tensor("poolprobe", [128, 64], mybir.dt.int8)
    probe_addr = _addr_of("poolprobe")
    limit = nc.sbuf_top - probe_addr - 64 - 4096
    POOL = None
    for sz in range(150 * 1024, limit + 1, 512):
        if _dry(sz):
            POOL = sz
            break
    assert POOL is not None, f"no pool size <= {limit} works"
    pool_h = nc.alloc_sbuf_tensor(pool_bytes_name, [128, POOL],
                                  mybir.dt.int8, align_bytes=64)
    pool_base = _addr_of(pool_bytes_name)
    assert pool_base is not None
    pa = PoolAlloc(pool_base, POOL)

    tensors = {}
    inherits = {}

    def palloc(name, shape, dtype, align=64):
        nbytes = shape[1] * (2 if dtype == BF16 else 4)
        assert nbytes == SZ[name], (name, nbytes, SZ[name])
        off, inh = pa.alloc(name, SZ[name], LIFE[name][0], LIFE[name][1],
                            align=align, side=SIDE.get(name, "left"))
        t = nc.alloc_sbuf_tensor_at(name, shape, dtype, offset=off)
        tensors[name] = t
        inherits[name] = inh
        return t

    bufreg = {}

    def pbuf(name):
        b = plan.buf(name, inherit=inherits.get(name, ()), reg=bufreg)
        bufreg.setdefault(name, []).append(b)
        return b

    wv_sb = palloc("wv", [128, DC * D], BF16)
    xbf1 = palloc("xbf1", [128, 2 * D], BF16)
    x_f = palloc("x_f", [128, DC * T], BF16)
    V_ext = palloc("V_ext", [128, TC * NP * 130], BF16)
    vals = palloc("vals", [128, NP * T], BF16)
    wo_sb = palloc("wo", [128, DC * D], BF16)
    pT = palloc("pT", [128, 5 * 512], BF16)
    q2 = palloc("q2", [128, 2048], BF16)
    prescan = palloc("prescan", [128, HF], BF16)
    qp = palloc("qp", [128, 2 * T], BF16)
    kp = palloc("kp", [128, 2 * T], BF16)
    wqs = palloc("wqs", [128, 2 * DC * 128], BF16)
    wks = palloc("wks", [128, 2 * DC * 128], BF16)
    acc = palloc("acc", [128, TC * D], F32)
    x1_f = palloc("x1_f", [128, DC * T], BF16)
    sqscratch = palloc("sqscratch", [128, D], F32)
    xchunk = palloc("xchunk", [128, 2 * D], F32)
    xbf2 = palloc("xbf2", [128, 3 * D], BF16)
    s1T = palloc("s1T", [128, 2 * DC * HG], BF16)
    s2T = palloc("s2T", [128, 2 * HGC * D], BF16)
    h_sb = palloc("h", [128, 2 * HGC * T], BF16)
    w1stag = palloc("w1stag", [128, HGC * D], BF16)
    w2stag = palloc("w2stag", [128, DC * HG], BF16)
    outstage = palloc("outstage", [128, 2 * D], F32)

    PB = [stack.enter_context(nc.psum_tensor(f"PB{i}", [128, 512], F32))
          for i in range(8)]

    sem_names = (list(ENG_SEM.values())
                 + ["d_init", "d_x0", "d_x1", "d_x2", "d_x3",
                    "d_wv", "d_q0", "d_q1",
                    "d_k0", "d_k1",
                    "d_wo", "d_pw", "d_f0", "d_f1", "d_w1", "d_w2",
                    "d_o0", "d_o1"])
    sems = {n: stack.enter_context(nc.semaphore(n)) for n in sem_names}

    def pbuf2(name):
        bufs = [plan.buf(name + "_0", inherit=inherits.get(name, ()),
                         reg=bufreg),
                plan.buf(name + "_1", inherit=inherits.get(name, ()),
                         reg=bufreg)]
        bufreg.setdefault(name, []).extend(bufs)
        return bufs

    b_consts = plan.buf("consts")
    b_wv = pbuf("wv")
    b_xf = pbuf("x_f")
    b_xbf1 = [plan.buf(f"xbf1_{i}", inherit=inherits.get("xbf1", ()),
                       reg=bufreg) for i in range(4)]
    bufreg.setdefault("xbf1", []).extend(b_xbf1)
    b_V = pbuf("V_ext")
    b_qp = pbuf2("qp"); b_kp = pbuf2("kp")
    b_wqs = pbuf2("wqs"); b_wks = pbuf2("wks")
    b_vals = pbuf("vals"); b_wo = pbuf("wo")
    b_pT = pbuf("pT"); b_q2 = pbuf("q2"); b_pre = pbuf("prescan")
    b_pTs = [plan.buf(f"pTs{c}", inherit=inherits.get("pT", ()),
                      reg=bufreg) for c in range(5)]
    bufreg.setdefault("pT", []).extend(b_pTs)
    b_x1f = pbuf("x1_f")
    b_acc = [plan.buf(f"acc{t}", inherit=inherits.get("acc", ()),
                      reg=bufreg) for t in range(TC)]
    bufreg.setdefault("acc", []).extend(b_acc)
    b_xc = pbuf2("xchunk")
    b_xbf2 = [plan.buf(f"xbf2_{i}", inherit=inherits.get("xbf2", ()),
                       reg=bufreg) for i in range(3)]
    bufreg.setdefault("xbf2", []).extend(b_xbf2)
    b_sq = pbuf("sqscratch")
    b_s1T = pbuf2("s1T")
    b_s2T = pbuf2("s2T")
    b_h = pbuf2("h")
    b_w1s = pbuf("w1stag"); b_w2s = pbuf("w2stag")
    b_out = pbuf2("outstage")
    b_PB = [plan.buf(f"PB{i}") for i in range(8)]
    b_biasc = plan.buf("bias_cols")
    b_a2 = plan.buf("alpha2")
    b_a1 = [plan.buf("alpha1g0"), plan.buf("alpha1g1")]
    b_qkmax = plan.buf("qkmax")
    b_nsm = plan.buf("nsm")
    b_negmb = plan.buf("negmb")
    b_invrow = [plan.buf("invrow"), plan.buf("invrow2")]
    b_invrowb = [plan.buf("invrowb"), plan.buf("invrowb2")]
    b_vtmp = [plan.buf("valstmp"), plan.buf("valstmp2")]
    b_qacc = plan.buf("qacc")
    b_bvw = plan.buf("bvwrow")
    b_bvTb = plan.buf("bvTb")
    b_ln2 = [plan.buf("lnS0"), plan.buf("lnS1")]
    b_bob = plan.buf("bob_buf")

    OP = plan.op

    for (dst, src) in [(ident, dident), (maskt, dmaskt), (maska, dmaska),
                       (onesall, dones),
                       (ones2, dones2), (bqT, dbqT), (bkT, dbkT),
                       (bvT, dbvT), (b1T, db1T), (b2b, db2b)]:
        OP("sync", (lambda d=dst, s=src: lambda e: e.dma_start(
            out=d[:, :], in_=s[:, :]))(), writes=[b_consts], dma="d_init")
    OP("sync", lambda e: e.dma_start(out=bob[:, :], in_=dbob[:, :]),
       writes=[b_bob], dma="d_init")

    def wrearr(d, cols):
        return d[:, :].rearrange("(c p) q -> p c q", p=128)

    def w3d(sb, cols=None):
        cols = D if cols is None else cols
        return sb[:, :].rearrange("p (c q) -> p c q", q=cols)

    DH2 = (DC + 1) // 2 * 128
    for tc in range(TC):
        sl = tc % 2
        for h in range(2):
            clo = h * DH2
            chi = min(D, (h + 1) * DH2)
            if clo >= chi:
                continue
            OP("gpsimd", (lambda tc=tc, sl=sl, clo=clo, chi=chi: lambda e:
                e.dma_start(
                    out=xbf1[:, sl * D + clo:sl * D + chi],
                    in_=dx[tc * 128:(tc + 1) * 128, clo:chi]))(),
               writes=[b_xbf1[2 * sl + h]], dma=f"d_x{2 * sl + h}")
        bank = b_PB[tc % 2]
        pbt = PB[tc % 2]
        for c in range(DC):
            hb = b_xbf1[2 * sl + (1 if c * 128 >= DH2 else 0)]
            OP("tensor", (lambda tc=tc, sl=sl, c=c, pbt=pbt: lambda e:
                e.matmul(pbt[:, (c % 4) * 128:(c % 4) * 128 + 128],
                         xbf1[:, sl * D + c * 128: sl * D + c * 128 + 128],
                         ident[:, :], start=True, stop=True))(),
               reads=[hb, b_consts], writes=[bank])
            if c % 4 == 3 or c == DC - 1:
                c0 = (c // 4) * 4
                n = c - c0 + 1
                OP("scalar", (lambda tc=tc, c0=c0, n=n, pbt=pbt: lambda e:
                    e.copy(x_f[:, :].rearrange("p (c t) -> p c t", t=T)
                           [:, c0:c0 + n, tc * 128:tc * 128 + 128],
                           pbt[:, :].rearrange("p (c t) -> p c t", t=128)
                           [:, 0:n, :]))(),
                   reads=[bank], writes=[b_xf])

    OP("gpsimd", lambda e: e.dma_start(out=w3d(wv_sb), in_=wrearr(dwv, D)),
       writes=[b_wv], dma="d_wv")
    OP("gpsimd", lambda e: e.dma_start(out=w3d(wo_sb), in_=wrearr(dwo, D)),
       writes=[b_wo], dma="d_wo")

    OP("vector", lambda e: e.memset(
        V_ext[:, :].rearrange("p (n s) -> p n s", s=65)[:, :, 64:65], 1.0),
       writes=[b_V])

    for tc in range(TC):
        par = tc % 2
        vb = [b_PB[2 * par], b_PB[2 * par + 1]]
        vp = [PB[2 * par], PB[2 * par + 1]]
        for c in range(DC):
            for j in range(NP):
                st = (c == 0 and j % 4 == 0)
                sp = (c == DC - 1 and (j % 4 == 3 or j == NP - 1))
                OP("tensor", (lambda tc=tc, c=c, j=j, vp=vp, st=st, sp=sp:
                    lambda e:
                    e.matmul(vp[j // 4][:, (j % 4) * 128:(j % 4) * 128 + 128],
                             x_f[:, c * T + tc * 128: c * T + tc * 128 + 128],
                             wv_sb[:, c * D + j * 128: c * D + j * 128
                                   + 128],
                             start=st, stop=sp, skip_group_check=True))(),
                   reads=[b_xf, b_wv],
                   writes=[vb[j // 4]])
        for j in range(NP):
            base = (tc * NP + j) * 130
            OP("vector", (lambda tc=tc, j=j, base=base, vp=vp: lambda e:
                e.tensor_copy(
                    V_ext[:, base:base + 130].rearrange(
                        "p (a b) -> p a b", b=65)[:, :, 0:64],
                    vp[j // 4][:, (j % 4) * 128:(j % 4) * 128 + 128]
                    .rearrange("p (a b) -> p a b", b=64)))(),
               reads=[vb[j // 4]], writes=[b_V])

    def prescan_dma(c):
        OP("gpsimd", (lambda c=c: lambda e: e.dma_start(
            out=prescan[:, :], in_=dw2[c * 128:(c + 1) * 128, :]))(),
           writes=[b_pre], dma="d_pw")

    def prescan_reduce(c, h):
        HH = HF // 2
        OP("vector", (lambda c=c, h=h, HH=HH: lambda e: e.tensor_reduce(
            alpha2[:, h * DC + c:h * DC + c + 1],
            prescan[:, h * HH:(h + 1) * HH], axis=AX.X, op=ALU.add,
            apply_absolute_value=True))(),
           reads=[b_pre], writes=[b_a2])


    QG = QGC
    GW = GWC
    WCH = max(1, 512 // GW)
    NU = TC // QG
    b_biasp = [plan.buf(f"biasc{j}") for j in range(NP)]

    def setup_items(j):
        sl = j % 2
        items = []
        for (srcb, src0, col0) in [(b_qp[sl], qp, 0), (b_kp[sl], kp, 1)]:
            def pre_sq(sl=sl, src0=src0, srcb=srcb):
                OP("vector", lambda e: e.tensor_mul(
                    q2[:, 0:T], src0[:, sl * T:(sl + 1) * T],
                    src0[:, sl * T:(sl + 1) * T]),
                   reads=[srcb], writes=[b_q2])
            items.append(("pre", pre_sq))
            for n in range(TN):
                w = min(512, T - n * 512)

                def mm_f(n=n, w=w):
                    def go(bi, slot):
                        OP("tensor", lambda e: e.matmul(
                            PB[bi][:, 0:w], ones2[:, :],
                            q2[:, n * 512:n * 512 + w],
                            start=True, stop=True),
                           reads=[b_q2, b_consts], writes=[b_PB[bi]])
                    return go

                def fin_f(n=n, w=w, col0=col0):
                    def go(bi, slot):
                        OP("vector", lambda e: e.reduce_max(
                            qkmax[:, col0 * TN + n:col0 * TN + n + 1],
                            PB[bi][:, 0:w], axis=AX.X),
                           reads=[b_PB[bi]], writes=[b_qkmax])
                    return go
                items.append(("task", mm_f(), fin_f()))

        def pre_chain_a(j=j):
            OP("vector", lambda e: e.reduce_max(
                nsm[:, 0:1], qkmax[:, 0:TN], axis=AX.X),
               reads=[b_qkmax], writes=[b_nsm])
            OP("vector", lambda e: e.reduce_max(
                nsm[:, 1:2], qkmax[:, TN:2 * TN], axis=AX.X),
               reads=[b_qkmax], writes=[b_nsm])
            OP("vector", lambda e: e.tensor_scalar(
                out=nsm[:, 2:3], in0=nsm[:, 0:1], scalar1=nsm[:, 1:2],
                scalar2=1e-30, op0=ALU.mult, op1=ALU.add),
               reads=[b_nsm], writes=[b_nsm])

        def pre_chain_b(j=j):
            OP("scalar", lambda e: e.activation(
                nsm[:, 3:4], nsm[:, 2:3], AF.Ln),
               reads=[b_nsm], writes=[b_nsm])
            OP("scalar", lambda e: e.activation(
                nsm[:, 4:5], nsm[:, 3:4], AF.Exp, scale=0.5),
               reads=[b_nsm], writes=[b_nsm])
            OP("vector", lambda e: e.tensor_scalar(
                out=negmb[:, 0:1], in0=nsm[:, 4:5], scalar1=-0.125,
                scalar2=None, op0=ALU.mult),
               reads=[b_nsm], writes=[b_negmb])
        items.append(("pre", pre_chain_a))
        items.append(("pre", pre_chain_b))

        def rep_mm(j=j):
            def go(bi, slot):
                def bb(e):
                    ins = None
                    for hp in range(2):
                        ins = e.matmul(
                            PB[bi][:, hp:hp + 1],
                            onesall[64 * hp:64 * hp + 1, 0:128],
                            negmb[64 * hp:64 * hp + 1, 0:1],
                            start=True, stop=True)
                    return ins
                OP("tensor", bb, reads=[b_negmb, b_consts],
                   writes=[b_PB[bi]])
            return go

        def rep_fin(j=j):
            def go(bi, slot):
                OP("vector", lambda e: e.tensor_copy(
                    bias_cols[:, 2 * j:2 * j + 2], PB[bi][:, 0:2]),
                   reads=[b_PB[bi]], writes=[b_biasp[j]])
            return go
        items.append(("task", rep_mm(), rep_fin()))
        return items

    def qk_items(j):
        sl = j % 2
        out = []
        for (wsl, bwsl, dst, bdst, bT) in [
                (wqs, b_wqs[sl], qp, b_qp[sl], bqT),
                (wks, b_wks[sl], kp, b_kp[sl], bkT)]:
            for n in range(TN):
                w = min(512, T - n * 512)

                halves = ([(0, DC, False)] if DC < 2 else
                          [(0, DC // 2, False), (DC // 2, DC, True)])
                for (c0, c1, fin_half) in halves:

                    def mk_mm(j=j, sl=sl, n=n, w=w, wsl=wsl, bwsl=bwsl,
                              c0=c0, c1=c1):
                        def go(bi, slot):
                            def bb(e):
                                ins = None
                                for c in range(c0, c1):
                                    ins = e.matmul(
                                        PB[bi][:, 0:w],
                                        wsl[:, sl * DC * 128 + c * 128:
                                            sl * DC * 128 + c * 128 + 128],
                                        x1f_src[:, c * T + n * 512:
                                                c * T + n * 512 + w],
                                        start=(c == c0),
                                        stop=(c == c1 - 1))
                                return ins
                            OP("tensor", bb, reads=[bwsl, b_xf],
                               writes=[b_PB[bi]])
                        return go

                    def mk_ev(j=j, sl=sl, n=n, w=w, dst=dst, bdst=bdst,
                              bT=bT, fin_half=fin_half, two=(DC >= 2)):
                        def go(bi, slot):
                            if two and not fin_half:
                                OP("vector", lambda e: e.tensor_copy(
                                    qacc[:, 0:w], PB[bi][:, 0:w]),
                                   reads=[b_PB[bi]], writes=[b_qacc])
                            elif two:
                                OP("vector", lambda e:
                                    e.scalar_tensor_tensor(
                                        out=dst[:, sl * T + n * 512:
                                                sl * T + n * 512 + w],
                                        in0=PB[bi][:, 0:w],
                                        scalar=bT[:, j:j + 1],
                                        in1=qacc[:, 0:w],
                                        op0=ALU.add, op1=ALU.add),
                                   reads=[b_PB[bi], b_consts, b_qacc],
                                   writes=[bdst])
                            else:
                                OP("vector", lambda e: e.tensor_scalar(
                                    out=dst[:, sl * T + n * 512:
                                            sl * T + n * 512 + w],
                                    in0=PB[bi][:, 0:w],
                                    scalar1=bT[:, j:j + 1],
                                    scalar2=None, op0=ALU.add),
                                   reads=[b_PB[bi], b_consts],
                                   writes=[bdst])
                        return go
                    out.append(("task", mk_mm(), mk_ev()))
        return out

    def wslice_load(j):
        sl = j % 2
        OP("gpsimd", (lambda j=j, sl=sl: lambda e: e.dma_start(
            out=wqs[:, sl * DC * 128:(sl + 1) * DC * 128].rearrange(
                "p (c q) -> p c q", q=128),
            in_=dwq[:, j * 128:(j + 1) * 128].rearrange(
                "(c p) q -> p c q", p=128)))(),
           writes=[b_wqs[sl]], dma=f"d_q{sl}")
        OP("gpsimd", (lambda j=j, sl=sl: lambda e: e.dma_start(
            out=wks[:, sl * DC * 128:(sl + 1) * DC * 128].rearrange(
                "p (c q) -> p c q", q=128),
            in_=dwk[:, j * 128:(j + 1) * 128].rearrange(
                "(c p) q -> p c q", p=128)))(),
           writes=[b_wks[sl]], dma=f"d_k{sl}")

    x1f_src = x_f
    wave_tasks = []
    pre_ops = {}

    def append_task(st_f, fin_f, epi_f, due_extra=1, lag=4):
        i = len(wave_tasks)
        bi = i % 4
        slot = i % 5
        if epi_f is not None and not isinstance(epi_f, tuple):
            epi_f = (epi_f,)
        wave_tasks.append((
            (lambda st_f=st_f, bi=bi, slot=slot: st_f(bi, slot)),
            (lambda fin_f=fin_f, bi=bi, slot=slot: fin_f(bi, slot)),
            epi_f if epi_f is None else
            tuple((lambda f=f, bi=bi, slot=slot: f(bi, slot))
                  for f in epi_f),
            due_extra, lag,
        ))
        return i

    pre_ops.setdefault(0, []).append(lambda: wslice_load(0))
    if NP > 1:
        pre_ops.setdefault(0, []).append(lambda: wslice_load(1))
    def append_dummy(n):
        for _ in range(n):
            append_task(lambda bi, slot: None, lambda bi, slot: None, None)

    from collections import deque

    def append_item(item):
        if item[0] == "pre":
            append_dummy(2)
            pre_ops.setdefault(len(wave_tasks), []).append(item[1])
        else:
            append_task(item[1], item[2], None, lag=2)

    for item in qk_items(0) + setup_items(0):
        append_item(item)
    append_dummy(4)

    for j in range(NP):
        inj = (deque(qk_items(j + 1) + setup_items(j + 1))
               if j + 1 < NP else deque())
        if j + 2 < NP:
            pre_ops.setdefault(len(wave_tasks) + 1, []).append(
                (lambda j=j: lambda: wslice_load(j + 2))())
        nw_j = 2 * sum((QG * (u + 1) + WCH - 1) // WCH for u in range(NU))
        lo_c, hi_c = j * DC // NP, (j + 1) * DC // NP
        nsub = 2 * (hi_c - lo_c) + 1
        for ci, c in enumerate(range(lo_c, hi_c)):
            off = max(3, (nw_j * (2 * ci + 1)) // (2 * nsub))
            pre_ops.setdefault(len(wave_tasks) + off, []).append(
                (lambda c=c: lambda: prescan_dma(c))())
            for h in range(2):
                offr = max(4, (nw_j * (2 * ci + 2 + h)) // (2 * nsub))
                pre_ops.setdefault(len(wave_tasks) + offr, []).append(
                    (lambda c=c, h=h: lambda: prescan_reduce(c, h))())
        import os as _os
        ntask = sum(1 for it in inj if it[0] == "task")
        stride = (10**9 if _os.environ.get('NOINJ') else
                  max(2, (2 * nw_j) // (3 * (ntask + 1)))) if inj else 0
        stride += stride % 2
        wcount = 0
        for u in range(NU):
            q0 = u * QG
            nch = q0 + QG
            nwav = (nch + WCH - 1) // WCH
            qcol = j * T + u * GW
            for w in range(nwav):
                l0 = w * WCH
                l1 = min(l0 + WCH, nch)
                qsl = j % 2

                def mk_st(l0=l0, l1=l1, j=j, qcol=qcol, q0=q0, qsl=qsl, u=u):
                    def go(bi0, bi1):
                        def bb(e):
                            ins = None
                            for l in range(l0, l1):
                                off = max(0, l - q0) * 128
                                for (hq, bb_) in ((0, bi0), (64, bi1)):
                                    ins = e.matmul(
                                        PB[bb_][:, (l - l0) * GW + off:
                                                (l - l0) * GW + GW],
                                        kp[hq:hq + 64,
                                           qsl * T + l * 128:
                                           qsl * T + l * 128 + 128],
                                        qp[hq:hq + 64,
                                           qsl * T + u * GW + off:
                                           qsl * T + u * GW + GW],
                                        start=True, stop=True)
                            return ins
                        OP("tensor", bb,
                           reads=[b_qp[qsl], b_kp[qsl]],
                           writes=[b_PB[bi0], b_PB[bi1]])
                    return go

                def mk_fin(l0=l0, l1=l1, j=j, q0=q0, nch=nch, qcol=qcol,
                           u=u, qsl=qsl, last=(w == nwav - 1)):
                    def go(hp, bi, slot):
                        st, stb = PB[bi], b_PB[bi]
                        pvx = 4 + hp + 2 * (u % 2)
                        pvp, pvb = PB[pvx], b_PB[pvx]
                        ptb = b_pTs[slot]
                        pt0 = slot * 512
                        regs = []
                        for l in range(l0, l1):
                            off = max(0, l - q0) * 128
                            if off == 0 and regs and regs[-1][1] == l:
                                regs[-1] = (regs[-1][0], l + 1, 0)
                            else:
                                regs.append((l, l + 1, off))
                        for (ra, rb, off) in regs:
                            OP("scalar", (lambda ra=ra, rb=rb, off=off,
                                          st=st, pt0=pt0, l0=l0, hp=hp:
                                lambda e: e.activation(
                                    pT[:, pt0 + (ra - l0) * GW + off:
                                       pt0 + (rb - l0) * GW],
                                    st[:, (ra - l0) * GW + off:
                                       (rb - l0) * GW],
                                    AF.Exp, scale=0.125,
                                    bias=bias_cols[:, 2 * j + hp:
                                                   2 * j + hp + 1]))(),
                               reads=[stb, b_biasp[j]], writes=[ptb])
                        for l in range(max(l0, q0), l1):
                            s = l - q0
                            OP("gpsimd", (lambda l=l, s=s, l0=l0, pt0=pt0:
                                lambda e: e.tensor_mul(
                                    pT[:, pt0 + (l - l0) * GW + s * 128:
                                       pt0 + (l - l0) * GW + s * 128 + 128],
                                    pT[:, pt0 + (l - l0) * GW + s * 128:
                                       pt0 + (l - l0) * GW + s * 128 + 128],
                                    maskt[:, :]))(),
                               reads=[b_consts], writes=[ptb])

                        def pv(e):
                            ins = None
                            for l in range(l0, l1):
                                off = max(0, l - q0) * 128
                                base = (l * NP + j) * 130 + hp * 65
                                ins = e.matmul(
                                    pvp[0:65, off:GW],
                                    V_ext[:, base:base + 65],
                                    pT[:, pt0 + (l - l0) * GW + off:
                                       pt0 + (l - l0) * GW + GW],
                                    start=(l == 0), stop=(l == nch - 1),
                                    skip_group_check=True)
                            return ins
                        OP("tensor", pv, reads=[b_V, ptb], writes=[pvb])

                    def epi_a(hp):
                        ir, irb, vt = (invrow, invrowb, valstmp) \
                            if hp == 0 else (invrow2, invrowb2, valstmp2)
                        pvx = 4 + hp + 2 * (u % 2)
                        pvp, pvb = PB[pvx], b_PB[pvx]
                        OP("vector", (lambda vt=vt, pvp=pvp: lambda e:
                            e.tensor_copy(
                                vt[0:64, 0:GW], pvp[0:64, 0:GW]))(),
                           reads=[pvb], writes=[b_vtmp[hp]])
                        OP("scalar", (lambda ir=ir, pvp=pvp: lambda e:
                            e.activation(
                                ir[64:65, 0:GW],
                                pvp[64:65, 0:GW], AF.Ln))(),
                           reads=[pvb], writes=[b_invrow[hp]])
                        OP("scalar", (lambda ir=ir, irb=irb: lambda e:
                            e.activation(
                                irb[64:65, 0:GW],
                                ir[64:65, 0:GW], AF.Exp, scale=-1.0))(),
                           reads=[b_invrow[hp]], writes=[b_invrowb[hp]])
                        OP("tensor", (lambda irb=irb, pvp=pvp: lambda e:
                            e.matmul(
                                pvp[0:64, 0:GW], onesall[64:65, 0:64],
                                irb[64:65, 0:GW], start=True, stop=True))(),
                           reads=[b_invrowb[hp], b_consts], writes=[pvb])

                    def epi_b(hp):
                        vt = valstmp if hp == 0 else valstmp2
                        pvx = 4 + hp + 2 * (u % 2)
                        pvp, pvb = PB[pvx], b_PB[pvx]
                        if hp == 0:
                            OP("vector", (lambda vt=vt, pvp=pvp: lambda e:
                                e.scalar_tensor_tensor(
                                    out=vals[0:64, qcol:qcol + GW],
                                    in0=vt[0:64, 0:GW], scalar=1.0,
                                    in1=pvp[0:64, 0:GW],
                                    op0=ALU.mult, op1=ALU.mult))(),
                               reads=[b_vtmp[hp], pvb], writes=[b_vals])
                        else:
                            OP("vector", (lambda vt=vt, pvp=pvp: lambda e:
                                e.scalar_tensor_tensor(
                                    out=vt[0:64, 0:GW],
                                    in0=vt[0:64, 0:GW], scalar=1.0,
                                    in1=pvp[0:64, 0:GW],
                                    op0=ALU.mult, op1=ALU.mult))(),
                               reads=[b_vtmp[hp], pvb],
                               writes=[b_vtmp[hp]])
                            OP("tensor", (lambda vt=vt, pvp=pvp: lambda e:
                                e.matmul(
                                    pvp[64:128, 0:GW], ident[0:64, 0:64],
                                    vt[0:64, 0:GW], start=True, stop=True,
                                    tile_position=(0, 64)))(),
                               reads=[b_vtmp[hp], b_consts], writes=[pvb])
                            OP("vector", (lambda pvp=pvp: lambda e:
                                e.tensor_copy(
                                    vals[64:128, qcol:qcol + GW],
                                    pvp[64:128, 0:GW]))(),
                               reads=[pvb], writes=[b_vals])
                    return go, ((epi_a, epi_b) if last else None)

                if u + 1 < NU:
                    nwav_next = (QG * (u + 2) + WCH - 1) // WCH
                else:
                    nwav_next = (QG + WCH - 1) // WCH
                due_extra = max(4, 2 * nwav_next - 2)
                fin_f, epi_f = mk_fin()
                i0 = len(wave_tasks)
                bi0, bi1 = i0 % 4, (i0 + 1) % 4
                st_pair = mk_st()
                append_task(lambda bi, slot: None,
                            (lambda fin_f=fin_f: lambda bi, slot:
                             fin_f(0, bi, slot))(),
                            None if epi_f is None else
                            tuple((lambda f=f: lambda bi, slot: f(0))()
                                  for f in epi_f), due_extra)
                append_task((lambda st_pair=st_pair, bi0=bi0, bi1=bi1:
                             lambda bi, slot: st_pair(bi0, bi1))(),
                            (lambda fin_f=fin_f: lambda bi, slot:
                             fin_f(1, bi, slot))(),
                            None if epi_f is None else
                            tuple((lambda f=f: lambda bi, slot: f(1))()
                                  for f in epi_f), due_extra)
                wcount += 2
                if inj and wcount % stride == 0:
                    append_item(inj.popleft())
                    while inj and inj[0][0] == "pre":
                        append_item(inj.popleft())
        if inj:
            while inj:
                append_item(inj.popleft())
            append_dummy(4)

    nwt = len(wave_tasks)
    epi_q = []

    def _fin(k, vi):
        wave_tasks[k][1]()
        epis = wave_tasks[k][2]
        if epis is not None:
            due = vi + wave_tasks[k][3]
            for si, f in enumerate(epis):
                epi_q.append((due - 2 * (len(epis) - 1 - si), f))

    def _flush(vi):
        i = 0
        while i < len(epi_q):
            if epi_q[i][0] <= vi:
                epi_q.pop(i)[1]()
            else:
                i += 1

    fin_due = {}
    for k in range(nwt):
        fin_due.setdefault(k + wave_tasks[k][4], []).append(k)
    last_due = max(fin_due) if fin_due else 0
    for i in range(max(nwt, last_due + 1)):
        for f in pre_ops.get(i, ()):
            f()
        for k in fin_due.get(i, ()):
            _fin(k, i)
        _flush(i)
        if i < nwt:
            wave_tasks[i][0]()
    for _, f in epi_q:
        f()

    OP("vector", lambda e: e.tensor_add(
        alpha2[:, 0:DC], alpha2[:, 0:DC], alpha2[:, DC:2 * DC]),
       reads=[b_a2], writes=[b_a2])
    OP("vector", lambda e: e.tensor_scalar(
        out=alpha2[:, 0:DC], in0=alpha2[:, 0:DC], scalar1=1.0 / HF,
        scalar2=None, op0=ALU.mult),
       reads=[b_a2], writes=[b_a2])

    OP("vector", lambda e: e.tensor_copy(bvTb[:, :], bvT[:, :]),
       reads=[b_consts], writes=[b_bvTb])
    for n in range(DN):
        w = min(512, D - n * 512)

        def mk_bvw(n=n, w=w):
            def go(e):
                ins = None
                for c in range(DC):
                    ins = e.matmul(
                        PB[6][0:1, 0:w], bvTb[:, c:c + 1],
                        wo_sb[:, c * D + n * 512: c * D + n * 512 + w],
                        start=(c == 0), stop=(c == DC - 1))
                return ins
            return go
        OP("tensor", mk_bvw(), reads=[b_bvTb, b_wo], writes=[b_PB[6]])
        OP("vector", (lambda n=n, w=w: lambda e: e.tensor_copy(
            bvwrow[0:1, n * 512:n * 512 + w], PB[6][0:1, 0:w]))(),
           reads=[b_PB[6]], writes=[b_bvw])
    for n in range(DN):
        w = min(512, D - n * 512)
        OP("tensor", (lambda n=n, w=w: lambda e: e.matmul(
            PB[6][:, 0:w], onesall[0:1, 0:128],
            bvwrow[0:1, n * 512:n * 512 + w], start=True, stop=True))(),
           reads=[b_bvw, b_consts], writes=[b_PB[6]])
        OP("vector", (lambda n=n, w=w: lambda e: e.tensor_add(
            bob[:, n * 512:n * 512 + w], bob[:, n * 512:n * 512 + w],
            PB[6][:, 0:w]))(),
           reads=[b_PB[6]], writes=[b_bob])


    def emit_lnstats(par, src, src_buf, mean_on_act=True):
        o = 8 * par
        bl = b_ln2[par]
        bs = b_sq
        if mean_on_act:
            OP("scalar", lambda e: e.activation(
                sqscratch[:, 0:D], src(), AF.Identity,
                accum_out=lnS[:, o:o + 1]),
               reads=[src_buf], writes=[bs, bl])
        else:
            OP("vector", lambda e: e.reduce_sum(
                lnS[:, o:o + 1], src(), axis=AX.X),
               reads=[src_buf], writes=[bl])
        OP("scalar", lambda e: e.activation(
            sqscratch[:, 0:D], src(), AF.Square,
            accum_out=lnS[:, o + 1:o + 2]),
           reads=[src_buf], writes=[bs, bl])
        OP("vector", lambda e: e.tensor_scalar(
            out=lnS[:, o + 2:o + 3], in0=lnS[:, o:o + 1], scalar1=1.0 / D,
            scalar2=None, op0=ALU.mult),
           reads=[bl], writes=[bl])
        OP("vector", lambda e: e.tensor_scalar(
            out=lnS[:, o + 3:o + 4], in0=lnS[:, o + 2:o + 3],
            scalar1=lnS[:, o + 2:o + 3], scalar2=None, op0=ALU.mult),
           reads=[bl], writes=[bl])
        OP("vector", lambda e: e.tensor_scalar(
            out=lnS[:, o + 4:o + 5], in0=lnS[:, o + 1:o + 2],
            scalar1=1.0 / D, scalar2=lnS[:, o + 3:o + 4], op0=ALU.mult,
            op1=ALU.subtract),
           reads=[bl], writes=[bl])
        OP("vector", lambda e: e.tensor_scalar(
            out=lnS[:, o + 5:o + 6], in0=lnS[:, o + 4:o + 5], scalar1=1e-5,
            scalar2=None, op0=ALU.add),
           reads=[bl], writes=[bl])
        OP("scalar", lambda e: e.activation(lnS[:, o + 6:o + 7],
                                            lnS[:, o + 5:o + 6], AF.Ln),
           reads=[bl], writes=[bl])
        OP("scalar", lambda e: e.activation(lnS[:, o + 7:o + 8],
                                            lnS[:, o + 6:o + 7], AF.Exp,
                                            scale=-0.5),
           reads=[bl], writes=[bl])
        return o, bl

    def emit_tr(tcq):
        s3 = tcq % 3
        for half in range(max(1, DC // 4)):
            bank = b_PB[4 + half % 2]
            pbt = PB[4 + half % 2]
            c0 = half * 4
            n = min(4, DC - c0)

            def mk_tr(s3=s3, c0=c0, n=n, pbt=pbt):
                def go(e):
                    ins = None
                    for k in range(n):
                        ins = e.matmul(
                            pbt[:, k * 128:k * 128 + 128],
                            xbf2[:, s3 * D + (c0 + k) * 128:
                                 s3 * D + (c0 + k) * 128 + 128],
                            ident[:, :], start=True, stop=True)
                    return ins
                return go
            OP("tensor", mk_tr(), reads=[b_xbf2[s3], b_consts],
               writes=[bank])
            OP("vector", (lambda tcq=tcq, c0=c0, n=n, pbt=pbt: lambda e:
                e.tensor_copy(x1_f[:, :].rearrange("p (c t) -> p c t", t=T)
                              [:, c0:c0 + n, tcq * 128:tcq * 128 + 128],
                              pbt[:, :].rearrange("p (c t) -> p c t", t=128)
                              [:, 0:n, :]))(),
               reads=[bank], writes=[b_x1f])

    for tcq in range(TC):
        sl = tcq % 2
        s3 = tcq % 3
        opair = (PB[2 * sl], PB[2 * sl + 1])
        obuf = (b_PB[2 * sl], b_PB[2 * sl + 1])
        OP("sync", (lambda tcq=tcq, sl=sl: lambda e: e.dma_start(
            out=xchunk[:, sl * D:(sl + 1) * D],
            in_=dx[tcq * 128:(tcq + 1) * 128, :]))(),
           writes=[b_xc[sl]], dma=f"d_f{sl}")
        OP("gpsimd", (lambda sl=sl: lambda e: e.tensor_add(
            xchunk[:, sl * D:(sl + 1) * D], xchunk[:, sl * D:(sl + 1) * D],
            bob[:, :]))(),
           reads=[b_xc[sl], b_bob], writes=[b_xc[sl]])

        def mk_o(tcq=tcq, opair=opair):
            def go(e):
                ins = None
                for c in range(DC):
                    for n in range(DN):
                        w = min(512, D - n * 512)
                        ins = e.matmul(
                            opair[n][:, 0:w],
                            vals[:, c * T + tcq * 128: c * T + tcq * 128
                                 + 128],
                            wo_sb[:, c * D + n * 512: c * D + n * 512
                                  + w],
                            start=(c == 0), stop=(c == DC - 1))
                return ins
            return go
        OP("tensor", mk_o(), reads=[b_vals, b_wo], writes=list(obuf[:DN]))
        if tcq >= 2:
            emit_tr(tcq - 2)
        for n in range(DN):
            w = min(512, D - n * 512)
            OP("vector", (lambda sl=sl, n=n, w=w, opair=opair: lambda e:
                e.tensor_add(
                    xchunk[:, sl * D + n * 512:sl * D + n * 512 + w],
                    xchunk[:, sl * D + n * 512:sl * D + n * 512 + w],
                    opair[n][:, 0:w]))(),
               reads=[b_xc[sl], obuf[n]], writes=[b_xc[sl]])
        o, bl = emit_lnstats(
            sl, (lambda sl=sl: lambda: xchunk[:, sl * D:(sl + 1) * D])(),
            b_xc[sl], mean_on_act=False)
        OP("vector", (lambda tcq=tcq, sl=sl, o=o: lambda e: e.tensor_scalar(
            out=acc[:, tcq * D:(tcq + 1) * D],
            in0=xchunk[:, sl * D:(sl + 1) * D], scalar1=lnS[:, o + 2:o + 3],
            scalar2=lnS[:, o + 7:o + 8], op0=ALU.subtract, op1=ALU.mult))(),
           reads=[b_xc[sl], bl], writes=[b_acc[tcq]])
        OP("vector", (lambda o=o: lambda e: e.tensor_scalar(
            out=lnS[:, o + 2:o + 3], in0=lnS[:, o + 2:o + 3],
            scalar1=lnS[:, o + 7:o + 8], scalar2=-1.0, op0=ALU.mult,
            op1=ALU.mult))(),
           reads=[bl], writes=[bl])
        OP("scalar", (lambda tcq=tcq, sl=sl, s3=s3, o=o: lambda e:
            e.activation(
                xbf2[:, s3 * D:(s3 + 1) * D], xchunk[:, sl * D:(sl + 1) * D],
                AF.Identity, bias=lnS[:, o + 2:o + 3],
                scale=lnS[:, o + 7:o + 8]))(),
           reads=[b_xc[sl], bl], writes=[b_xbf2[s3]])
    emit_tr(TC - 2)
    emit_tr(TC - 1)
    for tcq in range(TC):
        OP("vector", (lambda tcq=tcq: lambda e: e.tensor_add(
            acc[:, tcq * D:(tcq + 1) * D], acc[:, tcq * D:(tcq + 1) * D],
            b2b[:, :]))(),
           reads=[b_consts], writes=[b_acc[tcq]])

    def ffn_a1(g):
        sl = g % 2
        ab = sl * HGC
        OP("gpsimd", (lambda g=g: lambda e: e.dma_start(
            out=w1stag[:, :].rearrange("p (c d) -> p c d", d=D),
            in_=dw1[g * HG:(g + 1) * HG, :].rearrange(
                "(c p) d -> p c d", p=128)))(),
           writes=[b_w1s], dma="d_w1")
        for m in range(HGC):
            OP("vector", (lambda m=m, ab=ab: lambda e: e.tensor_reduce(
                alpha1g[:, ab + m:ab + m + 1], w1stag[:, m * D:(m + 1) * D],
                axis=AX.X, op=ALU.add, apply_absolute_value=True))(),
               reads=[b_w1s], writes=[b_a1[sl]])
        OP("vector", (lambda ab=ab: lambda e: e.tensor_scalar(
            out=alpha1g[:, ab:ab + HGC], in0=alpha1g[:, ab:ab + HGC],
            scalar1=1.0 / D, scalar2=None, op0=ALU.mult))(),
           reads=[b_a1[sl]], writes=[b_a1[sl]])
        OP("scalar", lambda e: e.activation(
            w1stag[:, :], w1stag[:, :], AF.Sign),
           reads=[b_w1s], writes=[b_w1s])
        OP("gpsimd", (lambda g=g: lambda e: e.dma_start(
            out=w2stag[:, :].rearrange("p (c h) -> p c h", h=HG),
            in_=dw2[:, :].rearrange("(c p) h -> p c h", p=128)
            [:, :, g * HG:(g + 1) * HG]))(),
           writes=[b_w2s], dma="d_w2")
        OP("scalar", lambda e: e.activation(
            w2stag[:, :], w2stag[:, :], AF.Sign),
           reads=[b_w2s], writes=[b_w2s])
        for c in range(DC):
            OP("vector", (lambda c=c: lambda e: e.tensor_scalar(
                out=w2stag[:, c * HG:(c + 1) * HG],
                in0=w2stag[:, c * HG:(c + 1) * HG],
                scalar1=alpha2[:, c:c + 1], scalar2=None, op0=ALU.mult))(),
               reads=[b_w2s, b_a2], writes=[b_w2s])

    def ffn_a2(g):
        sl = g % 2
        for c in range(DC):
            bank = b_PB[6 + c % 2]
            stp = PB[6 + c % 2]

            def mk_t1(c=c, stp=stp, sl=sl):
                def go(e):
                    ins = None
                    for m in range(HGC):
                        ins = e.matmul(
                            stp[:, m * 128:m * 128 + 128],
                            w1stag[:, m * D + c * 128: m * D + c * 128
                                   + 128],
                            ident[:, :], start=True, stop=True)
                    return ins
                return go
            OP("tensor", mk_t1(), reads=[b_w1s, b_consts], writes=[bank])
            OP("scalar", (lambda c=c, sl=sl, stp=stp: lambda e: e.copy(
                s1T[:, sl * DC * HG + c * HG: sl * DC * HG + c * HG
                    + HG],
                stp[:, 0:HG]))(),
               reads=[bank], writes=[b_s1T[sl]])
        for c in range(DC):
            bank = b_PB[6 + c % 2]
            stp = PB[6 + c % 2]

            def mk_t2(c=c, stp=stp):
                def go(e):
                    ins = None
                    for m in range(HGC):
                        ins = e.matmul(
                            stp[:, m * 128:m * 128 + 128],
                            w2stag[:, c * HG + m * 128: c * HG + m * 128
                                   + 128],
                            ident[:, :], start=True, stop=True)
                    return ins
                return go
            OP("tensor", mk_t2(), reads=[b_w2s, b_consts], writes=[bank])
            OP("scalar", (lambda c=c, sl=sl, stp=stp: lambda e: e.copy(
                s2T[:, :].rearrange("p (s m d) -> p s m d", s=2, m=HGC)
                [:, sl, :, c * 128:c * 128 + 128],
                stp[:, 0:HGC * 128].rearrange("p (m t) -> p m t", t=128)))(),
               reads=[bank], writes=[b_s2T[sl]])

    def ffn_mm1(g):
        sl = g % 2
        ab = sl * HGC
        for m in range(HGC):
            seta = (m % 2 == 0)
            banks = ([b_PB[0], b_PB[1], b_PB[2], b_PB[3]] if seta else
                     [b_PB[4], b_PB[5], b_PB[6], b_PB[7]])

            def psl(n, seta=seta):
                if seta:
                    return PB[n][:, 0:512]
                return PB[4 + n][:, 0:512]

            def mk_m1(m=m, sl=sl, seta=seta):
                def go(e):
                    ins = None
                    for c in range(DC):
                        for n in range(TN):
                            w = min(512, T - n * 512)
                            ins = e.matmul(
                                psl(n, seta)[:, 0:w],
                                s1T[:, sl * DC * HG + c * HG + m * 128:
                                    sl * DC * HG + c * HG + m * 128 + 128],
                                x1_f[:, c * T + n * 512: c * T + n * 512
                                     + w],
                                start=(c == 0), stop=(c == DC - 1))
                    return ins
                return go
            wb = sorted({id(banks[n]): banks[n] for n in range(TN)}.values(),
                        key=lambda b: b.name)
            OP("tensor", mk_m1(), reads=[b_s1T[sl], b_x1f], writes=wb)
            for n in range(TN):
                w = min(512, T - n * 512)
                OP("scalar", (lambda g=g, m=m, n=n, w=w, sl=sl, seta=seta,
                              ab=ab:
                    lambda e: e.activation(
                        h_sb[:, sl * HGC * T + m * T + n * 512:
                             sl * HGC * T + m * T + n * 512 + w],
                        psl(n, seta)[:, 0:w], AF.Gelu,
                        bias=b1T[:, g * HGC + m:g * HGC + m + 1],
                        scale=alpha1g[:, ab + m:ab + m + 1]))(),
                   reads=[banks[n], b_a1[sl], b_consts], writes=[b_h[sl]])

    def ffn_mm2(g, tail_cb=None):
        sl = g % 2
        for tc in range(TC):
            par = tc % 2
            obk = [b_PB[2 * par], b_PB[2 * par + 1]]
            opp = [PB[2 * par], PB[2 * par + 1]]

            def mk_m2(tc=tc, sl=sl, opp=opp):
                def go(e):
                    ins = None
                    for m in range(HGC):
                        for n in range(DN):
                            w = min(512, D - n * 512)
                            ins = e.matmul(
                                opp[n][:, 0:w],
                                h_sb[:, sl * HGC * T + m * T + tc * 128:
                                     sl * HGC * T + m * T + tc * 128 + 128],
                                s2T[:, sl * HGC * D + m * D + n * 512:
                                    sl * HGC * D + m * D + n * 512 + w],
                                start=(m == 0), stop=(m == HGC - 1))
                    return ins
                return go
            OP("tensor", mk_m2(), reads=[b_h[sl], b_s2T[sl]],
               writes=obk[:DN])
            for n in range(DN):
                w = min(512, D - n * 512)
                OP("vector", (lambda tc=tc, n=n, w=w, opp=opp: lambda e:
                    e.tensor_add(
                        acc[:, tc * D + n * 512: tc * D + n * 512 + w],
                        acc[:, tc * D + n * 512: tc * D + n * 512 + w],
                        opp[n][:, 0:w]))(),
                   reads=[obk[n]], writes=[b_acc[tc]])
            if tail_cb is not None:
                tail_cb(tc)

    def ln2_chunk(tc):
        sl = tc % 2
        o, bl = emit_lnstats(
            sl, (lambda tc=tc: lambda: acc[:, tc * D:(tc + 1) * D])(),
            b_acc[tc])
        OP("vector", (lambda tc=tc, sl=sl, o=o: lambda e: e.tensor_scalar(
            out=outstage[:, sl * D:(sl + 1) * D],
            in0=acc[:, tc * D:(tc + 1) * D], scalar1=lnS[:, o + 2:o + 3],
            scalar2=lnS[:, o + 7:o + 8], op0=ALU.subtract, op1=ALU.mult))(),
           reads=[b_acc[tc], bl], writes=[b_out[sl]])
        OP("sync", (lambda tc=tc, sl=sl: lambda e: e.dma_start(
            out=dy[tc * 128:(tc + 1) * 128, :],
            in_=outstage[:, sl * D:(sl + 1) * D]))(),
           reads=[b_out[sl]], dma=f"d_o{sl}")

    ffn_a1(0)
    ffn_a2(0)
    for g in range(NG):
        if g + 1 < NG:
            ffn_a1(g + 1)
        ffn_mm1(g)
        if g + 1 < NG:
            ffn_a2(g + 1)
        ffn_mm2(g, tail_cb=((lambda tc: ln2_chunk(tc - 2) if tc >= 2
                             else None) if g == NG - 1 else None))
    ln2_chunk(TC - 2)
    ln2_chunk(TC - 1)

    plan.final_waits = ["d_o0", "d_o1"]

    with nc.Block() as block:
        section = plan.lower(nc, block, sems)

        @block.sync
        def _(e):
            section("sync", e)

        @block.gpsimd
        def _(e):
            section("gpsimd", e)

        @block.tensor
        def _(e):
            section("tensor", e)

        @block.vector
        def _(e):
            section("vector", e)

        @block.scalar
        def _(e):
            section("scalar", e)

    stack.close()
    return nc


def host_consts(cfg):
    D = cfg["D"]; HF = cfg["HF"]; DC = D // 128; HFC = HF // 128
    bf = ml_dtypes.bfloat16
    ident = np.eye(128, dtype=bf)
    maskt = np.triu(np.ones((128, 128), dtype=np.float32)).astype(bf)
    maska = np.where(np.triu(np.ones((128, 128), dtype=np.float32)) > 0,
                     0.0, -1e9).astype(np.float32)
    onesall = np.ones((128, 128), dtype=bf)
    ones2 = np.zeros((128, 128), dtype=bf)
    ones2[0:64, 0] = 1
    ones2[64:128, 64] = 1
    return dict(ident=ident, maskt=maskt, maska=maska,
                onesall=onesall, ones2=ones2)


def make_in_maps(cfg, x, wq, bq, wk, bk, wv, bv, wo, bo, w1, b1, w2, b2):
    D = cfg["D"]; HF = cfg["HF"]; DC = D // 128; HFC = HF // 128
    n_cores = x.shape[0]
    f = np.ascontiguousarray
    cst = host_consts(cfg)
    shared = dict(
        wq=f(wq, dtype=np.float32), wk=f(wk, dtype=np.float32),
        wv=f(wv, dtype=np.float32), wo=f(wo, dtype=np.float32),
        w1=f(w1, dtype=np.float32), w2=f(w2, dtype=np.float32),
        bqT=f(np.asarray(bq, np.float32).reshape(DC, 128).T),
        bkT=f(np.asarray(bk, np.float32).reshape(DC, 128).T),
        bvT=f(np.asarray(bv, np.float32).reshape(DC, 128).T),
        b1T=f(np.asarray(b1, np.float32).reshape(HFC, 128).T),
        bob=f(np.broadcast_to(np.asarray(bo, np.float32), (128, D))),
        b2b=f(np.broadcast_to(np.asarray(b2, np.float32), (128, D))),
        **cst,
    )
    return [dict(x=f(x[i], dtype=np.float32), **shared)
            for i in range(n_cores)]


def kernel(L0_attn_x, L0_attn_W_q, L0_attn_b_q, L0_attn_W_k, L0_attn_b_k,
           L0_attn_W_v, L0_attn_b_v, L0_attn_W_o, L0_attn_b_o,
           L0_ffn_w1, L0_ffn_b1, L0_ffn_w2, L0_ffn_b2):
    cfg = dict(T=2048, D=1024, NH=16, HF=4096, HG=512)
    x = np.asarray(L0_attn_x, dtype=np.float32)
    n_cores = x.shape[0]
    nc = build_nc(cfg)
    in_maps = make_in_maps(cfg, x, L0_attn_W_q, L0_attn_b_q, L0_attn_W_k,
                           L0_attn_b_k, L0_attn_W_v, L0_attn_b_v,
                           L0_attn_W_o, L0_attn_b_o, L0_ffn_w1, L0_ffn_b1,
                           L0_ffn_w2, L0_ffn_b2)
    res = run_bass_kernel_spmd(nc, in_maps, core_ids=list(range(n_cores)))
    return np.stack([np.asarray(r["y"], dtype=np.float32)
                     for r in res.results])


# version 42 (final)
# speedup vs baseline: 1.0832x; 1.0051x over previous
import math
from contextlib import ExitStack

import numpy as np
import ml_dtypes

import concourse.bass as bass
import concourse.mybir as mybir
from concourse.bass_utils import run_bass_kernel_spmd

F32 = mybir.dt.float32
BF16 = mybir.dt.bfloat16
AF = mybir.ActivationFunctionType
ALU = mybir.AluOpType
AX = mybir.AxisListType

ENGS = ("sync", "gpsimd", "tensor", "vector", "scalar")
ENG_SEM = {"tensor": "pe", "vector": "dve", "scalar": "act", "sync": "sp",
           "gpsimd": "gp"}


class Buf:

    def __init__(self, plan, name, inherit=(), reg=None):
        self.plan = plan
        self.name = name
        self.writer = None
        self.readers = []
        self.inherit = list(inherit)
        self.reg = reg

    def resolve_inherit(self):
        deps = []
        if self.inherit and self.reg is not None:
            for nm in self.inherit:
                for ib in self.reg.get(nm, ()):
                    if ib.writer is not None:
                        deps.append(ib.writer)
                    deps.extend(ib.readers)
        self.inherit = []
        return deps


class Plan:
    def __init__(self):
        self.ops = []
        self.final_waits = []
        self._last_dma = {}

    def buf(self, name, inherit=(), reg=None):
        return Buf(self, name, inherit, reg)

    def op(self, eng, build, reads=(), writes=(), dma=None, extra=()):
        deps = set(extra)
        why = {}
        for b in reads:
            if b.writer is not None:
                deps.add(b.writer)
                why[b.writer] = "r:" + b.name
        for b in writes:
            for r in b.readers:
                why[r] = "war:" + b.name
            deps.update(b.readers)
            if b.writer is not None:
                deps.add(b.writer)
                why[b.writer] = "waw:" + b.name
            for d in b.resolve_inherit():
                deps.add(d)
                why[d] = "inh:" + b.name
        oid = len(self.ops)
        if dma is not None:
            prev = self._last_dma.get(dma)
            if prev is not None:
                deps.add(prev)
            self._last_dma[dma] = oid
        deps.discard(oid)
        for d in deps:
            assert d < oid
        self.ops.append(dict(eng=eng, build=build, deps=deps, dma=dma,
                             why=why))
        for b in reads:
            b.readers.append(oid)
        for b in writes:
            b.writer = oid
            b.readers = []
        return oid

    def lower(self, nc, block, sems):
        ops = self.ops
        waits = []
        marked = set()
        for i, o in enumerate(ops):
            w = []
            for d in o["deps"]:
                dd = ops[d]
                if (dd["dma"] is None and dd["eng"] == o["eng"]
                        and o["eng"] == "tensor"):
                    continue
                w.append(d)
            waits.append(w)
            marked.update(w)
        for i, o in enumerate(ops):
            if o["dma"] is not None:
                marked.add(i)
        rank = {}
        cnt = {}
        semof = {}
        for i, o in enumerate(ops):
            s = o["dma"] if o["dma"] is not None else ENG_SEM[o["eng"]]
            semof[i] = s
            if i in marked:
                cnt[s] = cnt.get(s, 0) + 1
                rank[i] = cnt[s]
        self._waits = waits
        self._marked = marked
        self._rank = rank
        self._semof = semof
        self._cnt = cnt

        def section(eng, engobj):
            last = {}
            for i, o in enumerate(ops):
                if o["eng"] != eng:
                    continue
                tgt = {}
                for d in self._waits[i]:
                    s = self._semof[d]
                    v = self._rank[d] * (16 if ops[d]["dma"] is not None else 1)
                    tgt[s] = max(tgt.get(s, 0), v)
                for s in sorted(tgt):
                    v = tgt[s]
                    if last.get(s, 0) < v:
                        engobj.wait_ge(sems[s], v)
                        last[s] = v
                ins = o["build"](engobj)
                if i in self._marked:
                    s = self._semof[i]
                    ins.then_inc(sems[s], 16 if o["dma"] is not None else 1)
            if eng == "sync":
                for s in self.final_waits:
                    tot = self._cnt.get(s, 0) * 16
                    if tot > last.get(s, 0):
                        engobj.wait_ge(sems[s], tot)

        return section


class PoolAlloc:

    def __init__(self, base, size):
        self.free = [(base, size)]
        self.live = {}
        self.hist = []

    def _expire(self, phase):
        dead = [n for n, (o, s, d) in self.live.items() if d < phase]
        for n in dead:
            off, size, d = self.live.pop(n)
            self.hist.append((off, size, n))
            self.free.append((off, size))
        self.free.sort()
        out = []
        for off, size in self.free:
            if out and out[-1][0] + out[-1][1] == off:
                out[-1] = (out[-1][0], out[-1][1] + size)
            else:
                out.append((off, size))
        self.free = [tuple(x) for x in out]

    def alloc(self, name, size, birth, death, align=64, side="left"):
        self._expire(birth)
        size = (size + align - 1) // align * align
        blocks = list(enumerate(self.free))
        if side == "right":
            blocks = blocks[::-1]
        for idx, (off, fsz) in blocks:
            if side == "right":
                a_off = (off + fsz - size) // align * align
                if a_off < off:
                    continue
            else:
                a_off = (off + align - 1) // align * align
            if a_off + size <= off + fsz:
                pre = a_off - off
                post = off + fsz - (a_off + size)
                repl = []
                if pre:
                    repl.append((off, pre))
                if post:
                    repl.append((a_off + size, post))
                self.free[idx:idx + 1] = repl
                inherit = [nm for h_off, h_sz, nm in self.hist
                           if h_off < a_off + size and a_off < h_off + h_sz]
                self.live[name] = (a_off, size, death)
                return a_off, inherit
        raise MemoryError(f"pool alloc failed for {name} size {size}")


def build_nc(cfg):
    T = cfg["T"]; D = cfg["D"]; NH = cfg["NH"]; HF = cfg["HF"]
    DH = 64
    TC = T // 128
    DC = D // 128
    NP = NH // 2
    TQ2 = TC // 2
    HG = cfg.get("HG", 512)
    NG = HF // HG
    HGC = HG // 128
    HFC = HF // 128
    TN = (T + 511) // 512
    DN = (D + 511) // 512
    NW = min(512, T)
    DW = min(512, D)
    WAVE = 4
    QGC = 4 if TC % 4 == 0 else 2
    GWC = QGC * 128
    assert NH * DH == D and T % 256 == 0 and D % 128 == 0 and HF % HG == 0
    assert HG % 128 == 0

    nc = bass.Bass("TRN2", target_bir_lowering=False)

    dx = nc.dram_tensor("x", [T, D], F32, kind="ExternalInput")
    dwq = nc.dram_tensor("wq", [D, D], F32, kind="ExternalInput")
    dwk = nc.dram_tensor("wk", [D, D], F32, kind="ExternalInput")
    dwv = nc.dram_tensor("wv", [D, D], F32, kind="ExternalInput")
    dwo = nc.dram_tensor("wo", [D, D], F32, kind="ExternalInput")
    dw1 = nc.dram_tensor("w1", [HF, D], F32, kind="ExternalInput")
    dw2 = nc.dram_tensor("w2", [D, HF], F32, kind="ExternalInput")
    dbqT = nc.dram_tensor("bqT", [128, DC], F32, kind="ExternalInput")
    dbkT = nc.dram_tensor("bkT", [128, DC], F32, kind="ExternalInput")
    dbvT = nc.dram_tensor("bvT", [128, DC], F32, kind="ExternalInput")
    db1T = nc.dram_tensor("b1T", [128, HFC], F32, kind="ExternalInput")
    dbob = nc.dram_tensor("bob", [128, D], F32, kind="ExternalInput")
    db2b = nc.dram_tensor("b2b", [128, D], F32, kind="ExternalInput")
    dident = nc.dram_tensor("ident", [128, 128], BF16, kind="ExternalInput")
    dmaskt = nc.dram_tensor("maskt", [128, 128], BF16, kind="ExternalInput")
    dmaska = nc.dram_tensor("maska", [128, 128], F32, kind="ExternalInput")
    dones = nc.dram_tensor("onesall", [128, 128], BF16, kind="ExternalInput")
    dones2 = nc.dram_tensor("ones2", [128, 128], BF16, kind="ExternalInput")
    dy = nc.dram_tensor("y", [T, D], F32, kind="ExternalOutput")

    stack = ExitStack()
    plan = Plan()

    def sball(name, shape, dtype):
        return nc.alloc_sbuf_tensor(name, shape, dtype)

    ident = sball("identS", [128, 128], BF16)
    maskt = sball("masktS", [128, 128], BF16)
    maska = sball("maskaS", [128, 128], F32)
    onesall = sball("onesallS", [128, 128], BF16)
    ones2 = sball("ones2S", [128, 128], BF16)
    bqT = sball("bqTS", [128, DC], F32)
    bkT = sball("bkTS", [128, DC], F32)
    bvT = sball("bvTS", [128, DC], F32)
    bvTb = sball("bvTbS", [128, DC], BF16)
    b1T = sball("b1TS", [128, HFC], F32)
    bob = sball("bobS", [128, D], F32)
    b2b = sball("b2bS", [128, D], F32)
    bias_cols = sball("biascolS", [128, NH], F32)
    alpha2 = sball("alpha2S", [128, 2 * DC], F32)
    alpha1g = sball("alpha1gS", [128, 2 * HGC], F32)
    qkmax = sball("qkmaxS", [128, 2 * max(TN, 1)], F32)
    nsm = sball("nsmS", [128, 8], F32)
    negmb = sball("negmbS", [128, 1], BF16)
    invrow = sball("invrowS", [128, 512], F32)
    invrow2 = sball("invrow2S", [128, 512], F32)
    invrowb = sball("invrowbS", [128, 512], BF16)
    invrowb2 = sball("invrowb2S", [128, 512], BF16)
    valstmp = sball("valstmpS", [128, 512], BF16)
    valstmp2 = sball("valstmp2S", [128, 512], BF16)
    qacc = sball("qaccS", [128, 512], F32)
    bvwrow = sball("bvwrowS", [128, D], BF16)
    lnS = sball("lnS", [128, 16], F32)

    pool_bytes_name = "pool0"

    def _addr_of(name):
        for a in nc.m.functions[0].allocations:
            try:
                for m in a.memorylocations:
                    if m.name == name:
                        return m.addr
            except Exception:
                pass
        return None

    SZ = dict(
        wv=DC * D * 2, qp=2 * T * 2, kp=2 * T * 2,
        wqs=2 * DC * 128 * 2, wks=2 * DC * 128 * 2,
        x_f=DC * T * 2, V_ext=TC * NP * 130 * 2,
        vals=NP * T * 2, wo=DC * D * 2,
        pT=5 * 512 * 2, q2=2048 * 2, prescan=HF * 2,
        acc=TC * D * 4, x1_f=DC * T * 2, xchunk=2 * D * 4,
        xbf1=2 * D * 2, xbf2=3 * D * 2, sqscratch=D * 4,
        s1T=2 * DC * HG * 2, s2T=2 * HGC * D * 2, h=2 * HGC * T * 2,
        w1stag=HGC * D * 2, w2stag=DC * HG * 2, outstage=2 * D * 4,
    )
    LIFE = dict(
        wv=(0, 2), qp=(5, 5), kp=(5, 5), wqs=(5, 5), wks=(5, 5),
        x_f=(1, 5), V_ext=(2, 5),
        vals=(5, 6), wo=(5, 6), pT=(5, 5),
        q2=(5, 5), prescan=(5, 5), acc=(6, 8), x1_f=(6, 7),
        xchunk=(6, 6), xbf1=(1, 1), xbf2=(6, 6), sqscratch=(6, 8),
        s1T=(7, 7), s2T=(7, 7), h=(7, 7), w1stag=(7, 7), w2stag=(7, 7),
        outstage=(8, 8),
    )
    import os as _os2
    if _os2.environ.get("DEBUG_LIVES"):
        LIFE = {k: (v[0], 8) for k, v in LIFE.items()}
    SIDE = dict(pT="right", q2="right", prescan="right", wo="right",
                xchunk="right", xbf1="right", xbf2="right",
                sqscratch="right", w1stag="right", w2stag="right",
                outstage="right", qp="right", kp="right", wqs="right",
                wks="right", wv="right", vals="right")
    ALLOC_ORDER = ["wv", "xbf1", "x_f", "V_ext",
                   "vals", "wo", "pT", "q2", "prescan",
                   "qp", "kp", "wqs", "wks",
                   "acc", "x1_f",
                   "sqscratch", "xchunk", "xbf2", "s1T", "s2T", "h",
                   "w1stag", "w2stag", "outstage"]

    def _dry(sz):
        p = PoolAlloc(0, sz)
        try:
            for n in ALLOC_ORDER:
                p.alloc(n, SZ[n], LIFE[n][0], LIFE[n][1],
                        side=SIDE.get(n, "left"))
        except MemoryError:
            return False
        return True

    probe = nc.alloc_sbuf_tensor("poolprobe", [128, 64], mybir.dt.int8)
    probe_addr = _addr_of("poolprobe")
    limit = nc.sbuf_top - probe_addr - 64 - 4096
    POOL = None
    for sz in range(150 * 1024, limit + 1, 512):
        if _dry(sz):
            POOL = sz
            break
    assert POOL is not None, f"no pool size <= {limit} works"
    pool_h = nc.alloc_sbuf_tensor(pool_bytes_name, [128, POOL],
                                  mybir.dt.int8, align_bytes=64)
    pool_base = _addr_of(pool_bytes_name)
    assert pool_base is not None
    pa = PoolAlloc(pool_base, POOL)

    tensors = {}
    inherits = {}

    def palloc(name, shape, dtype, align=64):
        nbytes = shape[1] * (2 if dtype == BF16 else 4)
        assert nbytes == SZ[name], (name, nbytes, SZ[name])
        off, inh = pa.alloc(name, SZ[name], LIFE[name][0], LIFE[name][1],
                            align=align, side=SIDE.get(name, "left"))
        t = nc.alloc_sbuf_tensor_at(name, shape, dtype, offset=off)
        tensors[name] = t
        inherits[name] = inh
        return t

    bufreg = {}

    def pbuf(name):
        b = plan.buf(name, inherit=inherits.get(name, ()), reg=bufreg)
        bufreg.setdefault(name, []).append(b)
        return b

    wv_sb = palloc("wv", [128, DC * D], BF16)
    xbf1 = palloc("xbf1", [128, 2 * D], BF16)
    x_f = palloc("x_f", [128, DC * T], BF16)
    V_ext = palloc("V_ext", [128, TC * NP * 130], BF16)
    vals = palloc("vals", [128, NP * T], BF16)
    wo_sb = palloc("wo", [128, DC * D], BF16)
    pT = palloc("pT", [128, 5 * 512], BF16)
    q2 = palloc("q2", [128, 2048], BF16)
    prescan = palloc("prescan", [128, HF], BF16)
    qp = palloc("qp", [128, 2 * T], BF16)
    kp = palloc("kp", [128, 2 * T], BF16)
    wqs = palloc("wqs", [128, 2 * DC * 128], BF16)
    wks = palloc("wks", [128, 2 * DC * 128], BF16)
    acc = palloc("acc", [128, TC * D], F32)
    x1_f = palloc("x1_f", [128, DC * T], BF16)
    sqscratch = palloc("sqscratch", [128, D], F32)
    xchunk = palloc("xchunk", [128, 2 * D], F32)
    xbf2 = palloc("xbf2", [128, 3 * D], BF16)
    s1T = palloc("s1T", [128, 2 * DC * HG], BF16)
    s2T = palloc("s2T", [128, 2 * HGC * D], BF16)
    h_sb = palloc("h", [128, 2 * HGC * T], BF16)
    w1stag = palloc("w1stag", [128, HGC * D], BF16)
    w2stag = palloc("w2stag", [128, DC * HG], BF16)
    outstage = palloc("outstage", [128, 2 * D], F32)

    PB = [stack.enter_context(nc.psum_tensor(f"PB{i}", [128, 512], F32))
          for i in range(8)]

    sem_names = (list(ENG_SEM.values())
                 + ["d_init", "d_x0", "d_x1", "d_x2", "d_x3",
                    "d_wv", "d_q0", "d_q1",
                    "d_k0", "d_k1",
                    "d_wo", "d_pw", "d_f0", "d_f1", "d_w1", "d_w2",
                    "d_o0", "d_o1"])
    sems = {n: stack.enter_context(nc.semaphore(n)) for n in sem_names}

    def pbuf2(name):
        bufs = [plan.buf(name + "_0", inherit=inherits.get(name, ()),
                         reg=bufreg),
                plan.buf(name + "_1", inherit=inherits.get(name, ()),
                         reg=bufreg)]
        bufreg.setdefault(name, []).extend(bufs)
        return bufs

    b_consts = plan.buf("consts")
    b_wv = pbuf("wv")
    b_xf = pbuf("x_f")
    b_xbf1 = [plan.buf(f"xbf1_{i}", inherit=inherits.get("xbf1", ()),
                       reg=bufreg) for i in range(4)]
    bufreg.setdefault("xbf1", []).extend(b_xbf1)
    b_V = pbuf("V_ext")
    b_qp = pbuf2("qp"); b_kp = pbuf2("kp")
    b_wqs = pbuf2("wqs"); b_wks = pbuf2("wks")
    b_vals = pbuf("vals"); b_wo = pbuf("wo")
    b_pT = pbuf("pT"); b_q2 = pbuf("q2"); b_pre = pbuf("prescan")
    b_pTs = [plan.buf(f"pTs{c}", inherit=inherits.get("pT", ()),
                      reg=bufreg) for c in range(5)]
    bufreg.setdefault("pT", []).extend(b_pTs)
    b_x1f = pbuf("x1_f")
    b_acc = [plan.buf(f"acc{t}", inherit=inherits.get("acc", ()),
                      reg=bufreg) for t in range(TC)]
    bufreg.setdefault("acc", []).extend(b_acc)
    b_xc = pbuf2("xchunk")
    b_xbf2 = [plan.buf(f"xbf2_{i}", inherit=inherits.get("xbf2", ()),
                       reg=bufreg) for i in range(3)]
    bufreg.setdefault("xbf2", []).extend(b_xbf2)
    b_sq = pbuf("sqscratch")
    b_s1T = pbuf2("s1T")
    b_s2T = pbuf2("s2T")
    b_h = pbuf2("h")
    b_w1s = pbuf("w1stag"); b_w2s = pbuf("w2stag")
    b_out = pbuf2("outstage")
    b_PB = [plan.buf(f"PB{i}") for i in range(8)]
    b_biasc = plan.buf("bias_cols")
    b_a2 = plan.buf("alpha2")
    b_a1 = [plan.buf("alpha1g0"), plan.buf("alpha1g1")]
    b_qkmax = plan.buf("qkmax")
    b_nsm = plan.buf("nsm")
    b_negmb = plan.buf("negmb")
    b_invrow = [plan.buf("invrow"), plan.buf("invrow2")]
    b_invrowb = [plan.buf("invrowb"), plan.buf("invrowb2")]
    b_vtmp = [plan.buf("valstmp"), plan.buf("valstmp2")]
    b_qacc = plan.buf("qacc")
    b_bvw = plan.buf("bvwrow")
    b_bvTb = plan.buf("bvTb")
    b_ln2 = [plan.buf("lnS0"), plan.buf("lnS1")]
    b_bob = plan.buf("bob_buf")

    OP = plan.op

    for (dst, src) in [(ident, dident), (maskt, dmaskt), (maska, dmaska),
                       (onesall, dones),
                       (ones2, dones2), (bqT, dbqT), (bkT, dbkT),
                       (bvT, dbvT), (b1T, db1T), (b2b, db2b)]:
        OP("sync", (lambda d=dst, s=src: lambda e: e.dma_start(
            out=d[:, :], in_=s[:, :]))(), writes=[b_consts], dma="d_init")
    OP("sync", lambda e: e.dma_start(out=bob[:, :], in_=dbob[:, :]),
       writes=[b_bob], dma="d_init")

    def wrearr(d, cols):
        return d[:, :].rearrange("(c p) q -> p c q", p=128)

    def w3d(sb, cols=None):
        cols = D if cols is None else cols
        return sb[:, :].rearrange("p (c q) -> p c q", q=cols)

    DH2 = (DC + 1) // 2 * 128
    for tc in range(TC):
        sl = tc % 2
        for h in range(2):
            clo = h * DH2
            chi = min(D, (h + 1) * DH2)
            if clo >= chi:
                continue
            OP("gpsimd", (lambda tc=tc, sl=sl, clo=clo, chi=chi: lambda e:
                e.dma_start(
                    out=xbf1[:, sl * D + clo:sl * D + chi],
                    in_=dx[tc * 128:(tc + 1) * 128, clo:chi]))(),
               writes=[b_xbf1[2 * sl + h]], dma=f"d_x{2 * sl + h}")
        bank = b_PB[tc % 2]
        pbt = PB[tc % 2]
        for c in range(DC):
            hb = b_xbf1[2 * sl + (1 if c * 128 >= DH2 else 0)]
            OP("tensor", (lambda tc=tc, sl=sl, c=c, pbt=pbt: lambda e:
                e.matmul(pbt[:, (c % 4) * 128:(c % 4) * 128 + 128],
                         xbf1[:, sl * D + c * 128: sl * D + c * 128 + 128],
                         ident[:, :], start=True, stop=True))(),
               reads=[hb, b_consts], writes=[bank])
            if c % 4 == 3 or c == DC - 1:
                c0 = (c // 4) * 4
                n = c - c0 + 1
                OP("scalar", (lambda tc=tc, c0=c0, n=n, pbt=pbt: lambda e:
                    e.copy(x_f[:, :].rearrange("p (c t) -> p c t", t=T)
                           [:, c0:c0 + n, tc * 128:tc * 128 + 128],
                           pbt[:, :].rearrange("p (c t) -> p c t", t=128)
                           [:, 0:n, :]))(),
                   reads=[bank], writes=[b_xf])

    OP("gpsimd", lambda e: e.dma_start(out=w3d(wv_sb), in_=wrearr(dwv, D)),
       writes=[b_wv], dma="d_wv")
    OP("gpsimd", lambda e: e.dma_start(out=w3d(wo_sb), in_=wrearr(dwo, D)),
       writes=[b_wo], dma="d_wo")

    OP("vector", lambda e: e.memset(
        V_ext[:, :].rearrange("p (n s) -> p n s", s=65)[:, :, 64:65], 1.0),
       writes=[b_V])

    for tc in range(TC):
        par = tc % 2
        vb = [b_PB[2 * par], b_PB[2 * par + 1]]
        vp = [PB[2 * par], PB[2 * par + 1]]
        for c in range(DC):
            for j in range(NP):
                st = (c == 0 and j % 4 == 0)
                sp = (c == DC - 1 and (j % 4 == 3 or j == NP - 1))
                OP("tensor", (lambda tc=tc, c=c, j=j, vp=vp, st=st, sp=sp:
                    lambda e:
                    e.matmul(vp[j // 4][:, (j % 4) * 128:(j % 4) * 128 + 128],
                             x_f[:, c * T + tc * 128: c * T + tc * 128 + 128],
                             wv_sb[:, c * D + j * 128: c * D + j * 128
                                   + 128],
                             start=st, stop=sp, skip_group_check=True))(),
                   reads=[b_xf, b_wv],
                   writes=[vb[j // 4]])
        for j in range(NP):
            base = (tc * NP + j) * 130
            OP("vector", (lambda tc=tc, j=j, base=base, vp=vp: lambda e:
                e.tensor_copy(
                    V_ext[:, base:base + 130].rearrange(
                        "p (a b) -> p a b", b=65)[:, :, 0:64],
                    vp[j // 4][:, (j % 4) * 128:(j % 4) * 128 + 128]
                    .rearrange("p (a b) -> p a b", b=64)))(),
               reads=[vb[j // 4]], writes=[b_V])

    def prescan_dma(c):
        OP("gpsimd", (lambda c=c: lambda e: e.dma_start(
            out=prescan[:, :], in_=dw2[c * 128:(c + 1) * 128, :]))(),
           writes=[b_pre], dma="d_pw")

    def prescan_reduce(c, h):
        HH = HF // 2
        OP("vector", (lambda c=c, h=h, HH=HH: lambda e: e.tensor_reduce(
            alpha2[:, h * DC + c:h * DC + c + 1],
            prescan[:, h * HH:(h + 1) * HH], axis=AX.X, op=ALU.add,
            apply_absolute_value=True))(),
           reads=[b_pre], writes=[b_a2])


    QG = QGC
    GW = GWC
    WCH = max(1, 512 // GW)
    NU = TC // QG
    b_biasp = [plan.buf(f"biasc{j}") for j in range(NP)]

    def setup_items(j):
        sl = j % 2
        items = []
        for (srcb, src0, col0) in [(b_qp[sl], qp, 0), (b_kp[sl], kp, 1)]:
            def pre_sq(sl=sl, src0=src0, srcb=srcb):
                OP("vector", lambda e: e.tensor_mul(
                    q2[:, 0:T], src0[:, sl * T:(sl + 1) * T],
                    src0[:, sl * T:(sl + 1) * T]),
                   reads=[srcb], writes=[b_q2])
            items.append(("pre", pre_sq))
            for n in range(TN):
                w = min(512, T - n * 512)

                def mm_f(n=n, w=w):
                    def go(bi, slot):
                        OP("tensor", lambda e: e.matmul(
                            PB[bi][:, 0:w], ones2[:, :],
                            q2[:, n * 512:n * 512 + w],
                            start=True, stop=True),
                           reads=[b_q2, b_consts], writes=[b_PB[bi]])
                    return go

                def fin_f(n=n, w=w, col0=col0):
                    def go(bi, slot):
                        OP("vector", lambda e: e.reduce_max(
                            qkmax[:, col0 * TN + n:col0 * TN + n + 1],
                            PB[bi][:, 0:w], axis=AX.X),
                           reads=[b_PB[bi]], writes=[b_qkmax])
                    return go
                items.append(("task", mm_f(), fin_f()))

        def pre_chain_a(j=j):
            OP("vector", lambda e: e.reduce_max(
                nsm[:, 0:1], qkmax[:, 0:TN], axis=AX.X),
               reads=[b_qkmax], writes=[b_nsm])
            OP("vector", lambda e: e.reduce_max(
                nsm[:, 1:2], qkmax[:, TN:2 * TN], axis=AX.X),
               reads=[b_qkmax], writes=[b_nsm])
            OP("vector", lambda e: e.tensor_scalar(
                out=nsm[:, 2:3], in0=nsm[:, 0:1], scalar1=nsm[:, 1:2],
                scalar2=1e-30, op0=ALU.mult, op1=ALU.add),
               reads=[b_nsm], writes=[b_nsm])

        def pre_chain_b(j=j):
            OP("scalar", lambda e: e.activation(
                nsm[:, 3:4], nsm[:, 2:3], AF.Ln),
               reads=[b_nsm], writes=[b_nsm])
            OP("scalar", lambda e: e.activation(
                nsm[:, 4:5], nsm[:, 3:4], AF.Exp, scale=0.5),
               reads=[b_nsm], writes=[b_nsm])
            OP("vector", lambda e: e.tensor_scalar(
                out=negmb[:, 0:1], in0=nsm[:, 4:5], scalar1=-0.125,
                scalar2=None, op0=ALU.mult),
               reads=[b_nsm], writes=[b_negmb])
        items.append(("pre", pre_chain_a))
        items.append(("pre", pre_chain_b))

        def rep_mm(j=j):
            def go(bi, slot):
                def bb(e):
                    ins = None
                    for hp in range(2):
                        ins = e.matmul(
                            PB[bi][:, hp:hp + 1],
                            onesall[64 * hp:64 * hp + 1, 0:128],
                            negmb[64 * hp:64 * hp + 1, 0:1],
                            start=True, stop=True)
                    return ins
                OP("tensor", bb, reads=[b_negmb, b_consts],
                   writes=[b_PB[bi]])
            return go

        def rep_fin(j=j):
            def go(bi, slot):
                OP("vector", lambda e: e.tensor_copy(
                    bias_cols[:, 2 * j:2 * j + 2], PB[bi][:, 0:2]),
                   reads=[b_PB[bi]], writes=[b_biasp[j]])
            return go
        items.append(("task", rep_mm(), rep_fin()))
        return items

    def qk_items(j):
        sl = j % 2
        out = []
        for (wsl, bwsl, dst, bdst, bT) in [
                (wqs, b_wqs[sl], qp, b_qp[sl], bqT),
                (wks, b_wks[sl], kp, b_kp[sl], bkT)]:
            for n in range(TN):
                w = min(512, T - n * 512)

                halves = ([(0, DC, False)] if DC < 2 else
                          [(0, DC // 2, False), (DC // 2, DC, True)])
                for (c0, c1, fin_half) in halves:

                    def mk_mm(j=j, sl=sl, n=n, w=w, wsl=wsl, bwsl=bwsl,
                              c0=c0, c1=c1):
                        def go(bi, slot):
                            def bb(e):
                                ins = None
                                for c in range(c0, c1):
                                    ins = e.matmul(
                                        PB[bi][:, 0:w],
                                        wsl[:, sl * DC * 128 + c * 128:
                                            sl * DC * 128 + c * 128 + 128],
                                        x1f_src[:, c * T + n * 512:
                                                c * T + n * 512 + w],
                                        start=(c == c0),
                                        stop=(c == c1 - 1))
                                return ins
                            OP("tensor", bb, reads=[bwsl, b_xf],
                               writes=[b_PB[bi]])
                        return go

                    def mk_ev(j=j, sl=sl, n=n, w=w, dst=dst, bdst=bdst,
                              bT=bT, fin_half=fin_half, two=(DC >= 2)):
                        def go(bi, slot):
                            if two and not fin_half:
                                OP("vector", lambda e: e.tensor_copy(
                                    qacc[:, 0:w], PB[bi][:, 0:w]),
                                   reads=[b_PB[bi]], writes=[b_qacc])
                            elif two:
                                OP("vector", lambda e:
                                    e.scalar_tensor_tensor(
                                        out=dst[:, sl * T + n * 512:
                                                sl * T + n * 512 + w],
                                        in0=PB[bi][:, 0:w],
                                        scalar=bT[:, j:j + 1],
                                        in1=qacc[:, 0:w],
                                        op0=ALU.add, op1=ALU.add),
                                   reads=[b_PB[bi], b_consts, b_qacc],
                                   writes=[bdst])
                            else:
                                OP("vector", lambda e: e.tensor_scalar(
                                    out=dst[:, sl * T + n * 512:
                                            sl * T + n * 512 + w],
                                    in0=PB[bi][:, 0:w],
                                    scalar1=bT[:, j:j + 1],
                                    scalar2=None, op0=ALU.add),
                                   reads=[b_PB[bi], b_consts],
                                   writes=[bdst])
                        return go
                    out.append(("task", mk_mm(), mk_ev()))
        return out

    def wslice_load(j):
        sl = j % 2
        OP("gpsimd", (lambda j=j, sl=sl: lambda e: e.dma_start(
            out=wqs[:, sl * DC * 128:(sl + 1) * DC * 128].rearrange(
                "p (c q) -> p c q", q=128),
            in_=dwq[:, j * 128:(j + 1) * 128].rearrange(
                "(c p) q -> p c q", p=128)))(),
           writes=[b_wqs[sl]], dma=f"d_q{sl}")
        OP("gpsimd", (lambda j=j, sl=sl: lambda e: e.dma_start(
            out=wks[:, sl * DC * 128:(sl + 1) * DC * 128].rearrange(
                "p (c q) -> p c q", q=128),
            in_=dwk[:, j * 128:(j + 1) * 128].rearrange(
                "(c p) q -> p c q", p=128)))(),
           writes=[b_wks[sl]], dma=f"d_k{sl}")

    x1f_src = x_f
    wave_tasks = []
    pre_ops = {}

    def append_task(st_f, fin_f, epi_f, due_extra=1, lag=4):
        i = len(wave_tasks)
        bi = i % 4
        slot = i % 5
        if epi_f is not None and not isinstance(epi_f, tuple):
            epi_f = (epi_f,)
        wave_tasks.append((
            (lambda st_f=st_f, bi=bi, slot=slot: st_f(bi, slot)),
            (lambda fin_f=fin_f, bi=bi, slot=slot: fin_f(bi, slot)),
            epi_f if epi_f is None else
            tuple((lambda f=f, bi=bi, slot=slot: f(bi, slot))
                  for f in epi_f),
            due_extra, lag,
        ))
        return i

    pre_ops.setdefault(0, []).append(lambda: wslice_load(0))
    if NP > 1:
        pre_ops.setdefault(0, []).append(lambda: wslice_load(1))
    def append_dummy(n):
        for _ in range(n):
            append_task(lambda bi, slot: None, lambda bi, slot: None, None)

    from collections import deque

    def append_item(item):
        if item[0] == "pre":
            append_dummy(2)
            pre_ops.setdefault(len(wave_tasks), []).append(item[1])
        else:
            append_task(item[1], item[2], None, lag=2)

    for item in qk_items(0) + setup_items(0):
        append_item(item)
    append_dummy(4)

    for j in range(NP):
        inj = (deque(qk_items(j + 1) + setup_items(j + 1))
               if j + 1 < NP else deque())
        if j + 2 < NP:
            pre_ops.setdefault(len(wave_tasks) + 1, []).append(
                (lambda j=j: lambda: wslice_load(j + 2))())
        nw_j = 2 * sum((QG * (u + 1) + WCH - 1) // WCH for u in range(NU))
        lo_c, hi_c = j * DC // NP, (j + 1) * DC // NP
        nsub = 2 * (hi_c - lo_c) + 1
        for ci, c in enumerate(range(lo_c, hi_c)):
            off = max(3, (nw_j * (2 * ci + 1)) // (2 * nsub))
            pre_ops.setdefault(len(wave_tasks) + off, []).append(
                (lambda c=c: lambda: prescan_dma(c))())
            for h in range(2):
                offr = max(4, (nw_j * (2 * ci + 2 + h)) // (2 * nsub))
                pre_ops.setdefault(len(wave_tasks) + offr, []).append(
                    (lambda c=c, h=h: lambda: prescan_reduce(c, h))())
        import os as _os
        ntask = sum(1 for it in inj if it[0] == "task")
        stride = (10**9 if _os.environ.get('NOINJ') else
                  max(2, (2 * nw_j) // (3 * (ntask + 1)))) if inj else 0
        stride += stride % 2
        wcount = 0
        for u in range(NU):
            q0 = u * QG
            nch = q0 + QG
            nwav = (nch + WCH - 1) // WCH
            qcol = j * T + u * GW
            for w in range(nwav):
                l0 = w * WCH
                l1 = min(l0 + WCH, nch)
                qsl = j % 2

                def mk_st(l0=l0, l1=l1, j=j, qcol=qcol, q0=q0, qsl=qsl, u=u):
                    def go(bi0, bi1):
                        def bb(e):
                            ins = None
                            for l in range(l0, l1):
                                off = max(0, l - q0) * 128
                                for (hq, bb_) in ((0, bi0), (64, bi1)):
                                    ins = e.matmul(
                                        PB[bb_][:, (l - l0) * GW + off:
                                                (l - l0) * GW + GW],
                                        kp[hq:hq + 64,
                                           qsl * T + l * 128:
                                           qsl * T + l * 128 + 128],
                                        qp[hq:hq + 64,
                                           qsl * T + u * GW + off:
                                           qsl * T + u * GW + GW],
                                        start=True, stop=True)
                            return ins
                        OP("tensor", bb,
                           reads=[b_qp[qsl], b_kp[qsl]],
                           writes=[b_PB[bi0], b_PB[bi1]])
                    return go

                def mk_fin(l0=l0, l1=l1, j=j, q0=q0, nch=nch, qcol=qcol,
                           u=u, qsl=qsl, last=(w == nwav - 1)):
                    def go(hp, bi, slot):
                        st, stb = PB[bi], b_PB[bi]
                        pvx = 4 + hp + 2 * (u % 2)
                        pvp, pvb = PB[pvx], b_PB[pvx]
                        ptb = b_pTs[slot]
                        pt0 = slot * 512
                        regs = []
                        for l in range(l0, l1):
                            off = max(0, l - q0) * 128
                            if off == 0 and regs and regs[-1][1] == l:
                                regs[-1] = (regs[-1][0], l + 1, 0)
                            else:
                                regs.append((l, l + 1, off))
                        for (ra, rb, off) in regs:
                            OP("scalar", (lambda ra=ra, rb=rb, off=off,
                                          st=st, pt0=pt0, l0=l0, hp=hp:
                                lambda e: e.activation(
                                    pT[:, pt0 + (ra - l0) * GW + off:
                                       pt0 + (rb - l0) * GW],
                                    st[:, (ra - l0) * GW + off:
                                       (rb - l0) * GW],
                                    AF.Exp, scale=0.125,
                                    bias=bias_cols[:, 2 * j + hp:
                                                   2 * j + hp + 1]))(),
                               reads=[stb, b_biasp[j]], writes=[ptb])
                        for l in range(max(l0, q0), l1):
                            s = l - q0
                            OP("gpsimd", (lambda l=l, s=s, l0=l0, pt0=pt0:
                                lambda e: e.tensor_mul(
                                    pT[:, pt0 + (l - l0) * GW + s * 128:
                                       pt0 + (l - l0) * GW + s * 128 + 128],
                                    pT[:, pt0 + (l - l0) * GW + s * 128:
                                       pt0 + (l - l0) * GW + s * 128 + 128],
                                    maskt[:, :]))(),
                               reads=[b_consts], writes=[ptb])

                        def pv(e):
                            ins = None
                            for l in range(l0, l1):
                                off = max(0, l - q0) * 128
                                base = (l * NP + j) * 130 + hp * 65
                                ins = e.matmul(
                                    pvp[0:65, off:GW],
                                    V_ext[:, base:base + 65],
                                    pT[:, pt0 + (l - l0) * GW + off:
                                       pt0 + (l - l0) * GW + GW],
                                    start=(l == 0), stop=(l == nch - 1),
                                    skip_group_check=True)
                            return ins
                        OP("tensor", pv, reads=[b_V, ptb], writes=[pvb])

                    def epi_a(hp):
                        ir, irb, vt = (invrow, invrowb, valstmp) \
                            if hp == 0 else (invrow2, invrowb2, valstmp2)
                        pvx = 4 + hp + 2 * (u % 2)
                        pvp, pvb = PB[pvx], b_PB[pvx]
                        OP("vector", (lambda vt=vt, pvp=pvp: lambda e:
                            e.tensor_copy(
                                vt[0:64, 0:GW], pvp[0:64, 0:GW]))(),
                           reads=[pvb], writes=[b_vtmp[hp]])
                        OP("scalar", (lambda ir=ir, pvp=pvp: lambda e:
                            e.activation(
                                ir[64:65, 0:GW],
                                pvp[64:65, 0:GW], AF.Ln))(),
                           reads=[pvb], writes=[b_invrow[hp]])
                        OP("scalar", (lambda ir=ir, irb=irb: lambda e:
                            e.activation(
                                irb[64:65, 0:GW],
                                ir[64:65, 0:GW], AF.Exp, scale=-1.0))(),
                           reads=[b_invrow[hp]], writes=[b_invrowb[hp]])
                        OP("tensor", (lambda irb=irb, pvp=pvp: lambda e:
                            e.matmul(
                                pvp[0:64, 0:GW], onesall[64:65, 0:64],
                                irb[64:65, 0:GW], start=True, stop=True))(),
                           reads=[b_invrowb[hp], b_consts], writes=[pvb])

                    def epi_b(hp):
                        vt = valstmp if hp == 0 else valstmp2
                        pvx = 4 + hp + 2 * (u % 2)
                        pvp, pvb = PB[pvx], b_PB[pvx]
                        if hp == 0:
                            OP("vector", (lambda vt=vt, pvp=pvp: lambda e:
                                e.scalar_tensor_tensor(
                                    out=vals[0:64, qcol:qcol + GW],
                                    in0=vt[0:64, 0:GW], scalar=1.0,
                                    in1=pvp[0:64, 0:GW],
                                    op0=ALU.mult, op1=ALU.mult))(),
                               reads=[b_vtmp[hp], pvb], writes=[b_vals])
                        else:
                            OP("vector", (lambda vt=vt, pvp=pvp: lambda e:
                                e.scalar_tensor_tensor(
                                    out=vt[0:64, 0:GW],
                                    in0=vt[0:64, 0:GW], scalar=1.0,
                                    in1=pvp[0:64, 0:GW],
                                    op0=ALU.mult, op1=ALU.mult))(),
                               reads=[b_vtmp[hp], pvb],
                               writes=[b_vtmp[hp]])
                            OP("tensor", (lambda vt=vt, pvp=pvp: lambda e:
                                e.matmul(
                                    pvp[64:128, 0:GW], ident[0:64, 0:64],
                                    vt[0:64, 0:GW], start=True, stop=True,
                                    tile_position=(0, 64)))(),
                               reads=[b_vtmp[hp], b_consts], writes=[pvb])
                            OP("vector", (lambda pvp=pvp: lambda e:
                                e.tensor_copy(
                                    vals[64:128, qcol:qcol + GW],
                                    pvp[64:128, 0:GW]))(),
                               reads=[pvb], writes=[b_vals])
                    return go, ((epi_a, epi_b) if last else None)

                if u + 1 < NU:
                    nwav_next = (QG * (u + 2) + WCH - 1) // WCH
                else:
                    nwav_next = (QG + WCH - 1) // WCH
                due_extra = max(4, 2 * nwav_next - 2)
                fin_f, epi_f = mk_fin()
                i0 = len(wave_tasks)
                bi0, bi1 = i0 % 4, (i0 + 1) % 4
                st_pair = mk_st()
                append_task(lambda bi, slot: None,
                            (lambda fin_f=fin_f: lambda bi, slot:
                             fin_f(0, bi, slot))(),
                            None if epi_f is None else
                            tuple((lambda f=f: lambda bi, slot: f(0))()
                                  for f in epi_f), due_extra)
                append_task((lambda st_pair=st_pair, bi0=bi0, bi1=bi1:
                             lambda bi, slot: st_pair(bi0, bi1))(),
                            (lambda fin_f=fin_f: lambda bi, slot:
                             fin_f(1, bi, slot))(),
                            None if epi_f is None else
                            tuple((lambda f=f: lambda bi, slot: f(1))()
                                  for f in epi_f), due_extra)
                wcount += 2
                if inj and wcount % stride == 0:
                    append_item(inj.popleft())
                    while inj and inj[0][0] == "pre":
                        append_item(inj.popleft())
        if inj:
            while inj:
                append_item(inj.popleft())
            append_dummy(4)

    nwt = len(wave_tasks)
    epi_q = []

    def _fin(k, vi):
        wave_tasks[k][1]()
        epis = wave_tasks[k][2]
        if epis is not None:
            due = vi + wave_tasks[k][3]
            for si, f in enumerate(epis):
                epi_q.append((due - 2 * (len(epis) - 1 - si), f))

    def _flush(vi):
        i = 0
        while i < len(epi_q):
            if epi_q[i][0] <= vi:
                epi_q.pop(i)[1]()
            else:
                i += 1

    fin_due = {}
    for k in range(nwt):
        fin_due.setdefault(k + wave_tasks[k][4], []).append(k)
    last_due = max(fin_due) if fin_due else 0
    for i in range(max(nwt, last_due + 1)):
        for f in pre_ops.get(i, ()):
            f()
        for k in fin_due.get(i, ()):
            _fin(k, i)
        _flush(i)
        if i < nwt:
            wave_tasks[i][0]()
    for _, f in epi_q:
        f()

    OP("vector", lambda e: e.tensor_add(
        alpha2[:, 0:DC], alpha2[:, 0:DC], alpha2[:, DC:2 * DC]),
       reads=[b_a2], writes=[b_a2])
    OP("vector", lambda e: e.tensor_scalar(
        out=alpha2[:, 0:DC], in0=alpha2[:, 0:DC], scalar1=1.0 / HF,
        scalar2=None, op0=ALU.mult),
       reads=[b_a2], writes=[b_a2])

    OP("vector", lambda e: e.tensor_copy(bvTb[:, :], bvT[:, :]),
       reads=[b_consts], writes=[b_bvTb])
    for n in range(DN):
        w = min(512, D - n * 512)

        def mk_bvw(n=n, w=w):
            def go(e):
                ins = None
                for c in range(DC):
                    ins = e.matmul(
                        PB[6][0:1, 0:w], bvTb[:, c:c + 1],
                        wo_sb[:, c * D + n * 512: c * D + n * 512 + w],
                        start=(c == 0), stop=(c == DC - 1))
                return ins
            return go
        OP("tensor", mk_bvw(), reads=[b_bvTb, b_wo], writes=[b_PB[6]])
        OP("vector", (lambda n=n, w=w: lambda e: e.tensor_copy(
            bvwrow[0:1, n * 512:n * 512 + w], PB[6][0:1, 0:w]))(),
           reads=[b_PB[6]], writes=[b_bvw])
    for n in range(DN):
        w = min(512, D - n * 512)
        OP("tensor", (lambda n=n, w=w: lambda e: e.matmul(
            PB[6][:, 0:w], onesall[0:1, 0:128],
            bvwrow[0:1, n * 512:n * 512 + w], start=True, stop=True))(),
           reads=[b_bvw, b_consts], writes=[b_PB[6]])
        OP("vector", (lambda n=n, w=w: lambda e: e.tensor_add(
            bob[:, n * 512:n * 512 + w], bob[:, n * 512:n * 512 + w],
            PB[6][:, 0:w]))(),
           reads=[b_PB[6]], writes=[b_bob])


    def emit_lnstats(par, src, src_buf, mean_on_act=True):
        o = 8 * par
        bl = b_ln2[par]
        bs = b_sq
        if mean_on_act:
            OP("scalar", lambda e: e.activation(
                sqscratch[:, 0:D], src(), AF.Identity,
                accum_out=lnS[:, o:o + 1]),
               reads=[src_buf], writes=[bs, bl])
        else:
            OP("vector", lambda e: e.reduce_sum(
                lnS[:, o:o + 1], src(), axis=AX.X),
               reads=[src_buf], writes=[bl])
        OP("scalar", lambda e: e.activation(
            sqscratch[:, 0:D], src(), AF.Square,
            accum_out=lnS[:, o + 1:o + 2]),
           reads=[src_buf], writes=[bs, bl])
        OP("vector", lambda e: e.tensor_scalar(
            out=lnS[:, o + 2:o + 3], in0=lnS[:, o:o + 1], scalar1=1.0 / D,
            scalar2=None, op0=ALU.mult),
           reads=[bl], writes=[bl])
        OP("vector", lambda e: e.tensor_scalar(
            out=lnS[:, o + 3:o + 4], in0=lnS[:, o + 2:o + 3],
            scalar1=lnS[:, o + 2:o + 3], scalar2=None, op0=ALU.mult),
           reads=[bl], writes=[bl])
        OP("vector", lambda e: e.tensor_scalar(
            out=lnS[:, o + 4:o + 5], in0=lnS[:, o + 1:o + 2],
            scalar1=1.0 / D, scalar2=lnS[:, o + 3:o + 4], op0=ALU.mult,
            op1=ALU.subtract),
           reads=[bl], writes=[bl])
        OP("vector", lambda e: e.tensor_scalar(
            out=lnS[:, o + 5:o + 6], in0=lnS[:, o + 4:o + 5], scalar1=1e-5,
            scalar2=None, op0=ALU.add),
           reads=[bl], writes=[bl])
        OP("scalar", lambda e: e.activation(lnS[:, o + 6:o + 7],
                                            lnS[:, o + 5:o + 6], AF.Ln),
           reads=[bl], writes=[bl])
        OP("scalar", lambda e: e.activation(lnS[:, o + 7:o + 8],
                                            lnS[:, o + 6:o + 7], AF.Exp,
                                            scale=-0.5),
           reads=[bl], writes=[bl])
        return o, bl

    def emit_tr(tcq):
        s3 = tcq % 3
        for half in range(max(1, DC // 4)):
            bank = b_PB[4 + half % 2]
            pbt = PB[4 + half % 2]
            c0 = half * 4
            n = min(4, DC - c0)

            def mk_tr(s3=s3, c0=c0, n=n, pbt=pbt):
                def go(e):
                    ins = None
                    for k in range(n):
                        ins = e.matmul(
                            pbt[:, k * 128:k * 128 + 128],
                            xbf2[:, s3 * D + (c0 + k) * 128:
                                 s3 * D + (c0 + k) * 128 + 128],
                            ident[:, :], start=True, stop=True)
                    return ins
                return go
            OP("tensor", mk_tr(), reads=[b_xbf2[s3], b_consts],
               writes=[bank])
            OP("vector", (lambda tcq=tcq, c0=c0, n=n, pbt=pbt: lambda e:
                e.tensor_copy(x1_f[:, :].rearrange("p (c t) -> p c t", t=T)
                              [:, c0:c0 + n, tcq * 128:tcq * 128 + 128],
                              pbt[:, :].rearrange("p (c t) -> p c t", t=128)
                              [:, 0:n, :]))(),
               reads=[bank], writes=[b_x1f])

    for tcq in range(TC):
        sl = tcq % 2
        s3 = tcq % 3
        opair = (PB[2 * sl], PB[2 * sl + 1])
        obuf = (b_PB[2 * sl], b_PB[2 * sl + 1])
        OP("sync", (lambda tcq=tcq, sl=sl: lambda e: e.dma_start(
            out=xchunk[:, sl * D:(sl + 1) * D],
            in_=dx[tcq * 128:(tcq + 1) * 128, :]))(),
           writes=[b_xc[sl]], dma=f"d_f{sl}")
        OP("gpsimd", (lambda sl=sl: lambda e: e.tensor_add(
            xchunk[:, sl * D:(sl + 1) * D], xchunk[:, sl * D:(sl + 1) * D],
            bob[:, :]))(),
           reads=[b_xc[sl], b_bob], writes=[b_xc[sl]])

        def mk_o(tcq=tcq, opair=opair):
            def go(e):
                ins = None
                for c in range(DC):
                    for n in range(DN):
                        w = min(512, D - n * 512)
                        ins = e.matmul(
                            opair[n][:, 0:w],
                            vals[:, c * T + tcq * 128: c * T + tcq * 128
                                 + 128],
                            wo_sb[:, c * D + n * 512: c * D + n * 512
                                  + w],
                            start=(c == 0), stop=(c == DC - 1))
                return ins
            return go
        OP("tensor", mk_o(), reads=[b_vals, b_wo], writes=list(obuf[:DN]))
        if tcq >= 2:
            emit_tr(tcq - 2)
        for n in range(DN):
            w = min(512, D - n * 512)
            OP("vector", (lambda sl=sl, n=n, w=w, opair=opair: lambda e:
                e.tensor_add(
                    xchunk[:, sl * D + n * 512:sl * D + n * 512 + w],
                    xchunk[:, sl * D + n * 512:sl * D + n * 512 + w],
                    opair[n][:, 0:w]))(),
               reads=[b_xc[sl], obuf[n]], writes=[b_xc[sl]])
        o, bl = emit_lnstats(
            sl, (lambda sl=sl: lambda: xchunk[:, sl * D:(sl + 1) * D])(),
            b_xc[sl], mean_on_act=False)
        OP("vector", (lambda tcq=tcq, sl=sl, o=o: lambda e: e.tensor_scalar(
            out=acc[:, tcq * D:(tcq + 1) * D],
            in0=xchunk[:, sl * D:(sl + 1) * D], scalar1=lnS[:, o + 2:o + 3],
            scalar2=lnS[:, o + 7:o + 8], op0=ALU.subtract, op1=ALU.mult))(),
           reads=[b_xc[sl], bl], writes=[b_acc[tcq]])
        OP("vector", (lambda o=o: lambda e: e.tensor_scalar(
            out=lnS[:, o + 2:o + 3], in0=lnS[:, o + 2:o + 3],
            scalar1=lnS[:, o + 7:o + 8], scalar2=-1.0, op0=ALU.mult,
            op1=ALU.mult))(),
           reads=[bl], writes=[bl])
        OP("scalar", (lambda tcq=tcq, sl=sl, s3=s3, o=o: lambda e:
            e.activation(
                xbf2[:, s3 * D:(s3 + 1) * D], xchunk[:, sl * D:(sl + 1) * D],
                AF.Identity, bias=lnS[:, o + 2:o + 3],
                scale=lnS[:, o + 7:o + 8]))(),
           reads=[b_xc[sl], bl], writes=[b_xbf2[s3]])
    emit_tr(TC - 2)
    emit_tr(TC - 1)
    for tcq in range(TC):
        OP("vector", (lambda tcq=tcq: lambda e: e.tensor_add(
            acc[:, tcq * D:(tcq + 1) * D], acc[:, tcq * D:(tcq + 1) * D],
            b2b[:, :]))(),
           reads=[b_consts], writes=[b_acc[tcq]])

    def ffn_a1(g):
        sl = g % 2
        ab = sl * HGC
        OP("gpsimd", (lambda g=g: lambda e: e.dma_start(
            out=w1stag[:, :].rearrange("p (c d) -> p c d", d=D),
            in_=dw1[g * HG:(g + 1) * HG, :].rearrange(
                "(c p) d -> p c d", p=128)))(),
           writes=[b_w1s], dma="d_w1")
        for m in range(HGC):
            OP("vector", (lambda m=m, ab=ab: lambda e: e.tensor_reduce(
                alpha1g[:, ab + m:ab + m + 1], w1stag[:, m * D:(m + 1) * D],
                axis=AX.X, op=ALU.add, apply_absolute_value=True))(),
               reads=[b_w1s], writes=[b_a1[sl]])
        OP("vector", (lambda ab=ab: lambda e: e.tensor_scalar(
            out=alpha1g[:, ab:ab + HGC], in0=alpha1g[:, ab:ab + HGC],
            scalar1=1.0 / D, scalar2=None, op0=ALU.mult))(),
           reads=[b_a1[sl]], writes=[b_a1[sl]])
        OP("scalar", lambda e: e.activation(
            w1stag[:, :], w1stag[:, :], AF.Sign),
           reads=[b_w1s], writes=[b_w1s])
        OP("gpsimd", (lambda g=g: lambda e: e.dma_start(
            out=w2stag[:, :].rearrange("p (c h) -> p c h", h=HG),
            in_=dw2[:, :].rearrange("(c p) h -> p c h", p=128)
            [:, :, g * HG:(g + 1) * HG]))(),
           writes=[b_w2s], dma="d_w2")
        OP("scalar", lambda e: e.activation(
            w2stag[:, :], w2stag[:, :], AF.Sign),
           reads=[b_w2s], writes=[b_w2s])
        for c in range(DC):
            OP("vector", (lambda c=c: lambda e: e.tensor_scalar(
                out=w2stag[:, c * HG:(c + 1) * HG],
                in0=w2stag[:, c * HG:(c + 1) * HG],
                scalar1=alpha2[:, c:c + 1], scalar2=None, op0=ALU.mult))(),
               reads=[b_w2s, b_a2], writes=[b_w2s])

    def ffn_a2(g):
        sl = g % 2
        for c in range(DC):
            bank = b_PB[6 + c % 2]
            stp = PB[6 + c % 2]

            def mk_t1(c=c, stp=stp, sl=sl):
                def go(e):
                    ins = None
                    for m in range(HGC):
                        ins = e.matmul(
                            stp[:, m * 128:m * 128 + 128],
                            w1stag[:, m * D + c * 128: m * D + c * 128
                                   + 128],
                            ident[:, :], start=True, stop=True)
                    return ins
                return go
            OP("tensor", mk_t1(), reads=[b_w1s, b_consts], writes=[bank])
            OP("scalar", (lambda c=c, sl=sl, stp=stp: lambda e: e.copy(
                s1T[:, sl * DC * HG + c * HG: sl * DC * HG + c * HG
                    + HG],
                stp[:, 0:HG]))(),
               reads=[bank], writes=[b_s1T[sl]])
        for c in range(DC):
            bank = b_PB[6 + c % 2]
            stp = PB[6 + c % 2]

            def mk_t2(c=c, stp=stp):
                def go(e):
                    ins = None
                    for m in range(HGC):
                        ins = e.matmul(
                            stp[:, m * 128:m * 128 + 128],
                            w2stag[:, c * HG + m * 128: c * HG + m * 128
                                   + 128],
                            ident[:, :], start=True, stop=True)
                    return ins
                return go
            OP("tensor", mk_t2(), reads=[b_w2s, b_consts], writes=[bank])
            OP("scalar", (lambda c=c, sl=sl, stp=stp: lambda e: e.copy(
                s2T[:, :].rearrange("p (s m d) -> p s m d", s=2, m=HGC)
                [:, sl, :, c * 128:c * 128 + 128],
                stp[:, 0:HGC * 128].rearrange("p (m t) -> p m t", t=128)))(),
               reads=[bank], writes=[b_s2T[sl]])

    def ffn_mm1(g):
        sl = g % 2
        ab = sl * HGC
        for m in range(HGC):
            seta = (m % 2 == 0)
            banks = ([b_PB[0], b_PB[1], b_PB[2], b_PB[3]] if seta else
                     [b_PB[4], b_PB[5], b_PB[6], b_PB[7]])

            def psl(n, seta=seta):
                if seta:
                    return PB[n][:, 0:512]
                return PB[4 + n][:, 0:512]

            def mk_m1(m=m, sl=sl, seta=seta):
                def go(e):
                    ins = None
                    for c in range(DC):
                        for n in range(TN):
                            w = min(512, T - n * 512)
                            ins = e.matmul(
                                psl(n, seta)[:, 0:w],
                                s1T[:, sl * DC * HG + c * HG + m * 128:
                                    sl * DC * HG + c * HG + m * 128 + 128],
                                x1_f[:, c * T + n * 512: c * T + n * 512
                                     + w],
                                start=(c == 0), stop=(c == DC - 1))
                    return ins
                return go
            wb = sorted({id(banks[n]): banks[n] for n in range(TN)}.values(),
                        key=lambda b: b.name)
            OP("tensor", mk_m1(), reads=[b_s1T[sl], b_x1f], writes=wb)
            for n in range(TN):
                w = min(512, T - n * 512)
                OP("scalar", (lambda g=g, m=m, n=n, w=w, sl=sl, seta=seta,
                              ab=ab:
                    lambda e: e.activation(
                        h_sb[:, sl * HGC * T + m * T + n * 512:
                             sl * HGC * T + m * T + n * 512 + w],
                        psl(n, seta)[:, 0:w], AF.Gelu,
                        bias=b1T[:, g * HGC + m:g * HGC + m + 1],
                        scale=alpha1g[:, ab + m:ab + m + 1]))(),
                   reads=[banks[n], b_a1[sl], b_consts], writes=[b_h[sl]])

    def ffn_mm2(g, tail_cb=None):
        sl = g % 2
        for tc in range(TC):
            par = tc % 3
            obk = [b_PB[2 * par], b_PB[2 * par + 1]]
            opp = [PB[2 * par], PB[2 * par + 1]]

            def mk_m2(tc=tc, sl=sl, opp=opp):
                def go(e):
                    ins = None
                    for m in range(HGC):
                        for n in range(DN):
                            w = min(512, D - n * 512)
                            ins = e.matmul(
                                opp[n][:, 0:w],
                                h_sb[:, sl * HGC * T + m * T + tc * 128:
                                     sl * HGC * T + m * T + tc * 128 + 128],
                                s2T[:, sl * HGC * D + m * D + n * 512:
                                    sl * HGC * D + m * D + n * 512 + w],
                                start=(m == 0), stop=(m == HGC - 1))
                    return ins
                return go
            OP("tensor", mk_m2(), reads=[b_h[sl], b_s2T[sl]],
               writes=obk[:DN])
            for n in range(DN):
                w = min(512, D - n * 512)
                OP("vector", (lambda tc=tc, n=n, w=w, opp=opp: lambda e:
                    e.tensor_add(
                        acc[:, tc * D + n * 512: tc * D + n * 512 + w],
                        acc[:, tc * D + n * 512: tc * D + n * 512 + w],
                        opp[n][:, 0:w]))(),
                   reads=[obk[n]], writes=[b_acc[tc]])
            if tail_cb is not None:
                tail_cb(tc)

    def ln2_chunk(tc):
        sl = tc % 2
        o, bl = emit_lnstats(
            sl, (lambda tc=tc: lambda: acc[:, tc * D:(tc + 1) * D])(),
            b_acc[tc])
        OP("vector", (lambda tc=tc, sl=sl, o=o: lambda e: e.tensor_scalar(
            out=outstage[:, sl * D:(sl + 1) * D],
            in0=acc[:, tc * D:(tc + 1) * D], scalar1=lnS[:, o + 2:o + 3],
            scalar2=lnS[:, o + 7:o + 8], op0=ALU.subtract, op1=ALU.mult))(),
           reads=[b_acc[tc], bl], writes=[b_out[sl]])
        OP("sync", (lambda tc=tc, sl=sl: lambda e: e.dma_start(
            out=dy[tc * 128:(tc + 1) * 128, :],
            in_=outstage[:, sl * D:(sl + 1) * D]))(),
           reads=[b_out[sl]], dma=f"d_o{sl}")

    ffn_a1(0)
    ffn_a2(0)
    for g in range(NG):
        if g + 1 < NG:
            ffn_a1(g + 1)
        ffn_mm1(g)
        if g + 1 < NG:
            ffn_a2(g + 1)
        ffn_mm2(g, tail_cb=((lambda tc: ln2_chunk(tc - 2) if tc >= 2
                             else None) if g == NG - 1 else None))
    ln2_chunk(TC - 2)
    ln2_chunk(TC - 1)

    plan.final_waits = ["d_o0", "d_o1"]

    with nc.Block() as block:
        section = plan.lower(nc, block, sems)

        @block.sync
        def _(e):
            section("sync", e)

        @block.gpsimd
        def _(e):
            section("gpsimd", e)

        @block.tensor
        def _(e):
            section("tensor", e)

        @block.vector
        def _(e):
            section("vector", e)

        @block.scalar
        def _(e):
            section("scalar", e)

    stack.close()
    return nc


def host_consts(cfg):
    D = cfg["D"]; HF = cfg["HF"]; DC = D // 128; HFC = HF // 128
    bf = ml_dtypes.bfloat16
    ident = np.eye(128, dtype=bf)
    maskt = np.triu(np.ones((128, 128), dtype=np.float32)).astype(bf)
    maska = np.where(np.triu(np.ones((128, 128), dtype=np.float32)) > 0,
                     0.0, -1e9).astype(np.float32)
    onesall = np.ones((128, 128), dtype=bf)
    ones2 = np.zeros((128, 128), dtype=bf)
    ones2[0:64, 0] = 1
    ones2[64:128, 64] = 1
    return dict(ident=ident, maskt=maskt, maska=maska,
                onesall=onesall, ones2=ones2)


def make_in_maps(cfg, x, wq, bq, wk, bk, wv, bv, wo, bo, w1, b1, w2, b2):
    D = cfg["D"]; HF = cfg["HF"]; DC = D // 128; HFC = HF // 128
    n_cores = x.shape[0]
    f = np.ascontiguousarray
    cst = host_consts(cfg)
    shared = dict(
        wq=f(wq, dtype=np.float32), wk=f(wk, dtype=np.float32),
        wv=f(wv, dtype=np.float32), wo=f(wo, dtype=np.float32),
        w1=f(w1, dtype=np.float32), w2=f(w2, dtype=np.float32),
        bqT=f(np.asarray(bq, np.float32).reshape(DC, 128).T),
        bkT=f(np.asarray(bk, np.float32).reshape(DC, 128).T),
        bvT=f(np.asarray(bv, np.float32).reshape(DC, 128).T),
        b1T=f(np.asarray(b1, np.float32).reshape(HFC, 128).T),
        bob=f(np.broadcast_to(np.asarray(bo, np.float32), (128, D))),
        b2b=f(np.broadcast_to(np.asarray(b2, np.float32), (128, D))),
        **cst,
    )
    return [dict(x=f(x[i], dtype=np.float32), **shared)
            for i in range(n_cores)]


def kernel(L0_attn_x, L0_attn_W_q, L0_attn_b_q, L0_attn_W_k, L0_attn_b_k,
           L0_attn_W_v, L0_attn_b_v, L0_attn_W_o, L0_attn_b_o,
           L0_ffn_w1, L0_ffn_b1, L0_ffn_w2, L0_ffn_b2):
    cfg = dict(T=2048, D=1024, NH=16, HF=4096, HG=512)
    x = np.asarray(L0_attn_x, dtype=np.float32)
    n_cores = x.shape[0]
    nc = build_nc(cfg)
    in_maps = make_in_maps(cfg, x, L0_attn_W_q, L0_attn_b_q, L0_attn_W_k,
                           L0_attn_b_k, L0_attn_W_v, L0_attn_b_v,
                           L0_attn_W_o, L0_attn_b_o, L0_ffn_w1, L0_ffn_b1,
                           L0_ffn_w2, L0_ffn_b2)
    res = run_bass_kernel_spmd(nc, in_maps, core_ids=list(range(n_cores)))
    return np.stack([np.asarray(r["y"], dtype=np.float32)
                     for r in res.results])
